# version 13
# baseline (speedup 1.0000x reference)
import numpy as np
import ml_dtypes
from contextlib import ExitStack

import concourse.bass as bass
import concourse.mybir as mybir
from concourse.bass_utils import run_bass_kernel_spmd

F32 = mybir.dt.float32
BF16 = mybir.dt.bfloat16
AF = mybir.ActivationFunctionType
ALU = mybir.AluOpType
AX = mybir.AxisListType
BF = ml_dtypes.bfloat16

N_CORES = 8
D = 1024
NH = 16
DH = 64
NPAIR = NH // 2
NCF = D // 128


def _apify(x):
    return x if isinstance(x, bass.AP) else x[:]


def _name(ap):
    return ap.tensor.name


_BAND = 1024


def _slots(ap):
    t = ap.tensor
    nm = t.name
    if "DRAM" in str(ap.space):
        return ((nm, -1),)
    apl = ap.ap
    esz = mybir.dt.size(ap.dtype)
    rowlen = apl[0][0]
    free_lo = ap.offset - ap.base_partition() * rowlen
    span = 1
    for step, count in apl[1:]:
        span += abs(step) * (count - 1)
    b0 = (free_lo * esz) // _BAND
    b1 = ((free_lo + span) * esz - 1) // _BAND
    return tuple((nm, b) for b in range(b0, b1 + 1))


class Tok:
    __slots__ = ("sem", "val", "used", "is_dma")

    def __init__(self, sem, val=None, is_dma=False):
        self.sem = sem
        self.val = val
        self.used = False
        self.is_dma = is_dma


class _Item:
    __slots__ = ("eng", "fn", "tok", "waits", "inc_amt", "do_inc")

    def __init__(self, eng, fn, tok, waits, inc_amt):
        self.eng = eng
        self.fn = fn
        self.tok = tok
        self.waits = waits
        self.inc_amt = inc_amt
        self.do_inc = False


class Prog:
    ENGINES = ("pe", "act", "dve", "pool", "sp")

    def __init__(self, nc, ctx):
        self.nc = nc
        self.ctx = ctx
        self.items = []
        self.w = {}
        self.r = {}
        self.sem_handles = {}
        self.stream_cnt = {}
        self.last_item = {}
        self.last_dma_tok = {}
        self.param_mode = False
        self._param_recs = []

    def _sem(self, key):
        if key not in self.sem_handles:
            self.sem_handles[key] = self.ctx.enter_context(
                self.nc.semaphore(f"s_{key}"))
        return self.sem_handles[key]

    def _emit(self, eng, fn, rd, wr, extra_waits=(), sem=None, dma=False):
        waits = list(extra_waits)
        rd = [s for a in rd if a is not None for s in _slots(a)]
        wr = [s for a in wr if a is not None for s in _slots(a)]
        for s in rd:
            t = self.w.get(s)
            if t is not None:
                waits.append(t)
        if dma and sem is None and self.param_mode:
            sem = "prmp" if eng == "pool" else "prm"
        semkey = sem if sem is not None else eng
        for s in wr:
            t = self.w.get(s)
            if t is not None and not (dma and t.is_dma and t.sem == semkey):
                waits.append(t)
            waits.extend(self.r.get(s, ()))
        if dma:
            cum = self.stream_cnt.get(semkey, 0) + 16
            self.stream_cnt[semkey] = cum
            tok = Tok(semkey, cum, is_dma=True)
            inc_amt = 16
            self.last_dma_tok[semkey] = tok
        else:
            tok = Tok(semkey)
            inc_amt = 1
        it = _Item(eng, fn, tok, waits, inc_amt)
        self.items.append(it)
        self.last_item[eng] = it
        for s in rd:
            self.r.setdefault(s, []).append(tok)
        for s in wr:
            self.w[s] = tok
            self.r[s] = []
        if dma and self.param_mode:
            self._param_recs.append((tok, wr))
        return tok

    def begin_params(self):
        self.param_mode = True
        self._param_recs = []

    def end_params(self):
        self.param_mode = False
        finals = {}
        for tok, wr in self._param_recs:
            finals[tok.sem] = tok
        for tok, wr in self._param_recs:
            for s in wr:
                if self.w.get(s) is tok:
                    self.w[s] = finals[tok.sem]
        self._param_recs = []

    def mm(self, out, lhsT, rhs, start=True, stop=True, tile_position=None):
        out, lhsT, rhs = _apify(out), _apify(lhsT), _apify(rhs)
        return self._emit(
            "pe", lambda t: t.matmul(out, lhsT, rhs, start=start, stop=stop,
                                     tile_position=tile_position),
            [lhsT, rhs], [out])

    def transp(self, out, in_, ident):
        out, in_, ident = _apify(out), _apify(in_), _apify(ident)
        return self._emit("pe", lambda t: t.transpose(out, in_, ident),
                          [in_, ident], [out])

    def act(self, out, in_, func, bias=0.0, scale=1.0, accum=None):
        out, in_ = _apify(out), _apify(in_)
        b = _apify(bias) if isinstance(bias, (bass.AP,)) or hasattr(bias, "tensor") or not isinstance(bias, (int, float)) else bias
        s = _apify(scale) if not isinstance(scale, (int, float)) else scale
        acc = _apify(accum) if accum is not None else None
        rd = [in_]
        if not isinstance(b, (int, float)):
            rd.append(b)
        if not isinstance(s, (int, float)):
            rd.append(s)
        wr = [out] + ([acc] if acc is not None else [])
        if acc is not None:
            return self._emit("act", lambda e: e.activation(
                out, in_, func, bias=b, scale=s, accum_out=acc), rd, wr)
        return self._emit("act", lambda e: e.activation(
            out, in_, func, bias=b, scale=s), rd, wr)

    def tt(self, out, in0, in1, op, eng="dve"):
        out, in0, in1 = _apify(out), _apify(in0), _apify(in1)
        return self._emit(eng, lambda e: e.tensor_tensor(
            out=out, in0=in0, in1=in1, op=op), [in0, in1], [out])

    def ts(self, out, in0, s1, s2=None, op0=ALU.mult, op1=ALU.bypass,
           eng="dve"):
        out, in0 = _apify(out), _apify(in0)
        s1a = _apify(s1) if not isinstance(s1, (int, float)) else s1
        s2a = _apify(s2) if s2 is not None and not isinstance(s2, (int, float)) else s2
        rd = [in0]
        if not isinstance(s1a, (int, float)):
            rd.append(s1a)
        if s2a is not None and not isinstance(s2a, (int, float)):
            rd.append(s2a)
        if s2 is None:
            return self._emit(eng, lambda e: e.tensor_single_scalar(
                out=out, in_=in0, scalar=s1a, op=op0), rd, [out])
        return self._emit(eng, lambda e: e.tensor_scalar(
            out=out, in0=in0, scalar1=s1a, scalar2=s2a, op0=op0, op1=op1),
            rd, [out])

    def stt(self, out, in0, s, in1, op0, op1, eng="dve"):
        out, in0, in1 = _apify(out), _apify(in0), _apify(in1)
        sa = _apify(s) if not isinstance(s, (int, float)) else s
        rd = [in0, in1] + ([sa] if not isinstance(sa, (int, float)) else [])
        return self._emit(eng, lambda e: e.scalar_tensor_tensor(
            out=out, in0=in0, scalar=sa, in1=in1, op0=op0, op1=op1),
            rd, [out])

    def copy(self, out, in_, eng="dve"):
        out, in_ = _apify(out), _apify(in_)
        return self._emit(eng, lambda e: e.tensor_copy(out=out, in_=in_),
                          [in_], [out])

    def red(self, out, in_, op, eng="dve", negate=None):
        out, in_ = _apify(out), _apify(in_)
        return self._emit(eng, lambda e: e.tensor_reduce(
            out=out, in_=in_, axis=AX.X, op=op, negate=negate), [in_], [out])

    def recip(self, out, in_):
        out, in_ = _apify(out), _apify(in_)
        return self._emit("dve", lambda e: e.reciprocal(out=out, in_=in_),
                          [in_], [out])

    def shuffle(self, out, in_, mask):
        out, in_ = _apify(out), _apify(in_)
        return self._emit("dve", lambda e: e.stream_shuffle(
            out=out, in_=in_, mask=mask), [in_], [out])

    def memset(self, ap, val, eng="pool"):
        ap = _apify(ap)
        return self._emit(eng, lambda e: e.memset(ap, val), [], [ap])

    def dma(self, queue, out, in_, sem=None):
        out, in_ = _apify(out), _apify(in_)
        if sem is None:
            sb = out if str(out.tensor.space).find("DRAM") < 0 else in_
            b0 = _slots(sb)[0][1]
            sem = ("dp_" if queue == "pool" else "d_") + _name(sb) + "_" + str(b0)
        return self._emit(queue, lambda e: e.dma_start(out=out, in_=in_),
                          [in_], [out], sem=sem, dma=True)

    FENCES_ON = False

    def fence(self):
        if not self.FENCES_ON:
            return
        toks = []
        for e in self.ENGINES:
            it = self.last_item.get(e)
            if it is not None:
                toks.append(it.tok)
        toks.extend(self.last_dma_tok.values())
        for e in self.ENGINES:
            self._emit(e, lambda eng: eng.nop(), [], [], extra_waits=toks)

    def finalize(self, final_toks=()):
        self._emit("sp", lambda e: e.nop(), [], [],
                   extra_waits=list(self.last_dma_tok.values()))
        for it in self.items:
            for t in it.waits:
                t.used = True
        CAP = 48000
        counts, seg = {}, {}
        for it in self.items:
            if not (it.tok.is_dma or it.tok.used):
                continue
            it.do_inc = True
            base = it.tok.sem
            amt = it.inc_amt
            if counts.get(base, 0) + amt > CAP:
                seg[base] = seg.get(base, 0) + 1
                counts[base] = 0
            c = counts.get(base, 0) + amt
            counts[base] = c
            it.tok.val = c
            if seg.get(base, 0):
                it.tok.sem = f"{base}@{seg[base]}"
        per_eng = {e: [] for e in self.ENGINES}
        for it in self.items:
            per_eng[it.eng].append(it)
        for it in self.items:
            self._sem(it.tok.sem)
            for t in it.waits:
                self._sem(t.sem)
        block = self.ctx.enter_context(self.nc.Block())
        guaranteed = {e: {} for e in self.ENGINES}

        def body(eng_name):
            def run(eng_obj):
                g = guaranteed[eng_name]
                for it in per_eng[eng_name]:
                    need = {}
                    for t in it.waits:
                        if t.val is None:
                            continue
                        if eng_name == "pe" and (t.sem == "pe" or
                                                 t.sem.startswith("pe@")):
                            continue
                        if need.get(t.sem, -1) < t.val:
                            need[t.sem] = t.val
                    for s, v in sorted(need.items()):
                        if g.get(s, -1) >= v:
                            continue
                        eng_obj.wait_ge(self._sem(s), v)
                        g[s] = v
                    inst = it.fn(eng_obj)
                    if it.do_inc and inst is not None:
                        inst.then_inc(self._sem(it.tok.sem), it.inc_amt)
            return run

        block.tensor(body("pe"))
        block.scalar(body("act"))
        block.vector(body("dve"))
        block.gpsimd(body("pool"))
        block.sync(body("sp"))


class BufSet:
    def __init__(self, nc, ctx):
        self.nc = nc
        self.ctx = ctx
        self.t = {}

    def add(self, name, shape, dtype):
        if name not in self.t:
            self.t[name] = self.ctx.enter_context(
                self.nc.sbuf_tensor("sb_" + name, shape, dtype))
        return self.t[name]

    def __getitem__(self, name):
        return self.t[name]


def make_common_bufs(nc, ctx, S):
    B = BufSet(nc, ctx)
    B.add("ident_b", [128, 128], BF16)
    B.add("ident_f", [128, 128], F32)
    B.add("ones1", [1, 128], BF16)
    B.add("ones1f", [1, 128], F32)
    B.add("onescol", [128, 1], BF16)
    B.add("ones2blk", [128, 2], BF16)
    B.add("xg", [128, NCF * 512], F32)
    B.add("xcb", [128, 512], BF16)
    B.add("sq2", [128, 512], BF16)
    B.add("hbuf", [128, NCF * 512], F32)
    B.add("wob", [128, NCF * D], BF16)
    B.add("xa_tA", [128, NCF * 512], BF16)
    B.add("xa_tB", [128, NCF * 512], BF16)
    B.add("at_t", [128, NCF * 512], BF16)
    SW = max(S, NCF * 512)
    B.add("qf", [128, SW], BF16)
    B.add("kf", [128, SW], BF16)
    B.add("vtm", [128, max(SW, 4352)], BF16)
    B.add("P0", [128, SW], BF16)
    B.add("P1", [128, SW], BF16)
    B.add("PT0", [128, SW], BF16)
    B.add("PT1", [128, SW], BF16)
    B.add("wqp", [128, D], BF16)
    B.add("wkp", [128, D], BF16)
    B.add("wvp", [128, D], BF16)
    B.add("t1", [128, 512], F32)
    B.add("t2", [128, 512], F32)
    B.add("t3", [128, 512], F32)
    B.t["pe_t"] = B.t["t3"]
    B.add("xab", [128, 512], BF16)
    B.add("x1o", [128, 512], F32)
    B.add("attn_t", [128, 128], F32)
    B.add("attn_o", [128, 128], BF16)
    for nm in ("m_row", "q_row", "t_row", "r_row", "mr_row"):
        B.add(nm, [1, 512], F32)
    for nm in ("r_row_hi", "r_row_lo", "mr_row_hi", "mr_row_lo"):
        B.add(nm, [1, 512], BF16)
    B.add("bv_row", [1, D], F32)
    B.add("bvr_hi", [1, D], BF16)
    B.add("bvr_lo", [1, D], BF16)
    B.add("sums", [128, 16], F32)
    B.add("scol", [128, 2], F32)
    B.add("rcol", [128, 2], F32)
    B.add("rcol_hi", [128, 2], BF16)
    B.add("rcol_lo", [128, 2], BF16)
    for i in range(4):
        B.add("rrow%d" % i, [1, 128], BF16)
    B.add("bcol", [128, 2], F32)
    B.add("triumask", [128, 128], F32)
    B.add("dnr", [1, 128], F32)
    B.add("dnrh", [1, 128], BF16)
    B.add("dnrl", [1, 128], BF16)
    B.t["numt"] = B.t["attn_t"]
    B.add("nrmt", [2, 16], F32)
    B.add("nrm", [2, 2], F32)
    B.add("nrm_s", [2, 2], F32)
    B.add("bneg", [2, 1], F32)
    B.add("bnrow", [1, 2], BF16)
    B.add("bneg_b", [2, 1], BF16)
    for nm in ("bq_c", "bk_c", "bv_c", "bo_c", "gamma_c", "beta_c"):
        B.add("pc_" + nm, [128, NCF], F32)
    B.add("trilmask", [128, 128], BF16)
    B.add("ropeA_t", [128, 512], F32)
    B.add("ropeB_t", [128, 512], F32)
    B.add("swp", [128, 512], BF16)
    B.add("qro", [128, 512], BF16)
    B.add("g8", [128, 8], F32)
    B.add("g8b", [128, 8], F32)
    B.add("m1c", [128, 1], F32)
    B.add("m2c", [128, 1], F32)
    B.add("gk", [128, 8], F32)
    B.add("cTx", [32, 512], F32)
    B.add("cT8b", [8, 512], BF16)
    B.add("crow", [1, 512], F32)
    B.add("crh", [1, 512], BF16)
    B.add("crl", [1, 512], BF16)
    B.add("gbh", [1, 8], BF16)
    B.add("gbl", [1, 8], BF16)
    B.add("gbrow", [1, 8], F32)
    B.add("expb8", [8, D], BF16)
    for nm in ("ln1w_c", "ln1b_c", "ln2w_c", "ln2b_c"):
        B.add("pc_" + nm, [128, NCF], F32)
    return B


def make_psum(nc, ctx):
    psf = [ctx.enter_context(nc.psum_tensor(f"psf{i}", [128, 512], F32))
           for i in range(6)]
    psb = [ctx.enter_context(nc.psum_tensor(f"psb{i}", [128, 1024], BF16))
           for i in range(2)]
    return psf, psb


CONST_NAMES = ("ident_b", "ident_f", "ones1", "ones1f", "onescol",
               "ones2blk")


def load_consts(p, B, dram):
    p.begin_params()
    for nm in CONST_NAMES:
        p.dma("pool", B[nm], dram[nm])
    p.end_params()


class Converter:

    UCOLS = 2048

    def __init__(self, p, B):
        self.p = p
        self.B = B
        self.units = []
        self.k = 0

    def add(self, src2d, dst2d):
        R, C = src2d.tensor.shape[-2], src2d.tensor.shape[-1]
        if C > self.UCOLS:
            for r in range(0, R, 128):
                for c0 in range(0, C, self.UCOLS):
                    self.units.append(
                        (src2d[r:r + 128, c0:c0 + self.UCOLS],
                         dst2d[r:r + 128, c0:c0 + self.UCOLS]))
        else:
            rc = (128 * self.UCOLS) // C
            for r in range(0, R, rc):
                self.units.append((src2d[r:r + rc, :], dst2d[r:r + rc, :]))

    def pump(self, n):
        for _ in range(n):
            if self.k >= len(self.units):
                return
            src, dst = self.units[self.k]
            half = self.k % 2
            stg = self.B["cvstg"][:, self.UCOLS * half:
                                  self.UCOLS * (half + 1)]
            self.p.dma("pool", stg,
                       src.rearrange("(t p) n -> p t n", p=128))
            self.p.dma("sp", dst.rearrange("(t p) n -> p t n", p=128),
                       stg, sem="cvout")
            self.k += 1

    def done(self):
        self.pump(len(self.units) - self.k)


CV = None


def cv_pump(n):
    if CV is not None:
        CV.pump(n)


def load_group(p, B, dram_x, cols, buf="xg", queue="sp"):
    p.dma(queue, B[buf][:, 0:NCF * 512],
          dram_x[:, cols].rearrange("(t p) n -> p t n", p=128))


def load_2d(p, dst_ap, dram_2d, queue="pool"):
    p.dma(queue, dst_ap, dram_2d.rearrange("(t p) n -> p t n", p=128))


def row_stats(p, B, sum_ap, sumsq_ap, eps, n=512, extra_scale=None):
    m = B["m_row"][0:1, 0:n]
    q = B["q_row"][0:1, 0:n]
    t = B["t_row"][0:1, 0:n]
    r = B["r_row"][0:1, 0:n]
    mr = B["mr_row"][0:1, 0:n]
    p.ts(m, sum_ap, 1.0 / D)
    p.ts(q, sumsq_ap, 1.0 / D)
    p.tt(t, m, m, ALU.mult)
    p.stt(q, q, eps, t, ALU.add, ALU.subtract)
    p.act(t, q, AF.Sqrt)
    p.recip(r, t)
    if extra_scale is not None:
        extra_scale(p, B, n)
    p.tt(mr, m, r, ALU.mult)
    for nm in ("r_row", "mr_row"):
        src = B[nm][0:1, 0:n]
        p.copy(B[nm + "_hi"][0:1, 0:n], src)
        p.tt(B[nm + "_lo"][0:1, 0:n], src, B[nm + "_hi"][0:1, 0:n],
             ALU.subtract)


def bcast_rows(p, B, bank_ap, hi_ap, lo_ap):
    p.mm(bank_ap, B["ones1"], hi_ap, start=True, stop=False)
    p.mm(bank_ap, B["ones1"], lo_ap, start=False, stop=True)


def emit_block_a(p, B, psf, psb, dram, S):
    NQ = S // 128
    NKC = S // 512
    NTG = S // 512

    p.begin_params()
    for nm in ("bq_c", "bk_c", "bv_c", "bo_c", "gamma_c", "beta_c"):
        p.dma("sp", B["pc_" + nm], dram[nm])
    p.dma("sp", B["bv_row"], dram["bv_r"])
    p.end_params()
    p.copy(B["bvr_hi"], B["bv_row"])
    p.tt(B["bvr_lo"], B["bv_row"], B["bvr_hi"], ALU.subtract)

    for g in range(NTG):
        cols = slice(512 * g, 512 * (g + 1))
        load_group(p, B, dram["x_in"], cols)
        ln_stats_group(p, B, psf, "xg", eps=1e-5)
        bcast_rows(p, B, psf[2][:, :], B["r_row_hi"][0:1, :],
                   B["r_row_lo"][0:1, :])
        bcast_rows(p, B, psf[3][:, :], B["mr_row_hi"][0:1, :],
                   B["mr_row_lo"][0:1, :])
        for c in range(NCF):
            sl = slice(512 * c, 512 * (c + 1))
            rows = slice(128 * c, 128 * (c + 1))
            p.dma("sp", B["pe_t"], dram["peT"][rows, cols])
            p.tt(B["t1"], B["xg"][:, sl], psf[2][:, :], ALU.mult)
            p.tt(B["t2"], B["t1"], psf[3][:, :], ALU.subtract)
            p.tt(B["xab"], B["t2"], B["pe_t"], ALU.add)
            p.dma("sp", dram["xa_sp"][rows, cols], B["xab"])
    p.fence()

    init_vtm_ones(p, B, S)
    for pr in range(NPAIR):
        pcols = slice(128 * pr, 128 * (pr + 1))
        for wn in ("wq", "wk", "wv"):
            load_2d(p, B[wn + "p"][:, 0:D], dram[wn][:, pcols], "pool")
        for j in range(S // 512):
            jc = slice(512 * j, 512 * (j + 1))
            xa_t = B["xa_tA"] if j % 2 == 0 else B["xa_tB"]
            load_2d(p, xa_t[:, 0:NCF * 512], dram["xa_sp"][:, jc], "act")
            for c in range(NCF):
                xac = xa_t[:, 512 * c:512 * (c + 1)]
                p.mm(psf[0], B["wqp"][:, 128 * c:128 * (c + 1)], xac,
                     start=(c == 0), stop=(c == NCF - 1))
                p.mm(psf[1], B["wkp"][:, 128 * c:128 * (c + 1)], xac,
                     start=(c == 0), stop=(c == NCF - 1))
            p.ts(B["qf"][:, jc], psf[0][:, :], B["pc_bq_c"][:, pr:pr + 1],
                 op0=ALU.add)
            p.ts(B["kf"][:, jc], psf[1][:, :], B["pc_bk_c"][:, pr:pr + 1],
                 op0=ALU.add)
            for tt_ in range(4):
                kt = 4 * j + tt_
                for c in range(NCF):
                    p.mm(psf[2][:, 0:128],
                         xa_t[:, 512 * c + 128 * tt_:512 * c + 128 * (tt_ + 1)],
                         B["wvp"][:, 128 * c:128 * (c + 1)],
                         start=(c == 0), stop=False)
                p.mm(psf[2][:, 0:128], B["ones1"],
                     B["bvr_hi"][0:1, pcols], start=False, stop=False)
                p.mm(psf[2][:, 0:128], B["ones1"],
                     B["bvr_lo"][0:1, pcols], start=False, stop=True)
                p.copy(B["vtm"][:, 132 * kt:132 * kt + 64], psf[2][:, 0:64])
                p.copy(B["vtm"][:, 132 * kt + 66:132 * kt + 130],
                       psf[2][:, 64:128])
        emit_score_bound(p, B, psf, S)
        emit_attn_qtile_T(p, B, psf, psb, dram, S, pr, causal=False)
        cv_pump(8)
    p.fence()

    load_2d(p, B["wob"][:, 0:NCF * D], dram["wo"][:, :], "pool")
    for j in range(S // 512):
        jc = slice(512 * j, 512 * (j + 1))
        atb = B["at_t"] if j % 2 == 0 else B["qf"]
        xab_ = B["xa_tA"] if j % 2 == 0 else B["kf"]
        load_2d(p, atb[:, 0:NCF * 512], dram["at_sp"][:, jc], "act")
        load_2d(p, xab_[:, 0:NCF * 512], dram["xa_sp"][:, jc], "sp")
        load_group(p, B, dram["x_in"], jc, queue="act")
        for oc in range(NCF):
            bank = psf[oc % 4]
            for c in range(NCF):
                p.mm(bank,
                     B["wob"][:, D * c + 128 * oc:D * c + 128 * (oc + 1)],
                     atb[:, 512 * c:512 * (c + 1)],
                     start=(c == 0), stop=(c == NCF - 1))
            p.ts(B["t1"], bank[:, :], B["pc_bo_c"][:, oc:oc + 1],
                 op0=ALU.add)
            p.tt(B["hbuf"][:, 512 * oc:512 * (oc + 1)], B["t1"],
                 xab_[:, 512 * oc:512 * (oc + 1)], ALU.add)
        ln_stats_group(p, B, psf, "hbuf", eps=1e-5)
        bcast_rows(p, B, psf[4][:, :], B["r_row_hi"][0:1, :],
                   B["r_row_lo"][0:1, :])
        bcast_rows(p, B, psf[5][:, :], B["mr_row_hi"][0:1, :],
                   B["mr_row_lo"][0:1, :])
        for oc in range(NCF):
            sl = slice(512 * oc, 512 * (oc + 1))
            rows = slice(128 * oc, 128 * (oc + 1))
            p.tt(B["t1"], B["hbuf"][:, sl], psf[4][:, :], ALU.mult)
            p.tt(B["t2"], B["t1"], psf[5][:, :], ALU.subtract)
            p.ts(B["t3"], B["t2"], B["pc_gamma_c"][:, oc:oc + 1],
                 B["pc_beta_c"][:, oc:oc + 1], op0=ALU.mult, op1=ALU.add)
            p.tt(B["x1o"], B["t3"], B["xg"][:, sl], ALU.add)
            p.dma("sp", dram["x_out"][rows, jc], B["x1o"])
        cv_pump(8)
    p.fence()


def init_vtm_ones(p, B, S):
    for kt in range(S // 128):
        p.memset(B["vtm"][:, 132 * kt + 64:132 * kt + 65], 1.0, eng="dve")
        p.memset(B["vtm"][:, 132 * kt + 130:132 * kt + 131], 1.0, eng="dve")


def emit_attn_qtile_T(p, B, psf, psb, dram, S, pr, causal):
    NKT = S // 128
    sc_banks = [(psf[0][:, :], psf[1][:, :]),
                (psb[0][:].bitcast(F32)[:, 0:512],
                 psb[1][:].bitcast(F32)[:, 0:512])]
    for qg in range(S // 512):
        qsl = slice(512 * qg, 512 * (qg + 1))
        nkt = (4 * qg + 4) if causal else NKT

        def scores(kt):
            ksl = slice(128 * kt, 128 * (kt + 1))
            b0, b1 = sc_banks[kt % 2]
            p.mm(b0, B["kf"][0:64, ksl], B["qf"][0:64, qsl])
            p.mm(b1, B["kf"][64:128, ksl], B["qf"][64:128, qsl])

        def expmask(kt):
            b0, b1 = sc_banks[kt % 2]
            loc = kt - 4 * qg
            po = 512 * (kt % 4)
            for hh, bb in ((0, b0), (1, b1)):
                PT = B["PT%d" % hh]
                if causal and loc >= 0:
                    z = 128 * loc
                    if z:
                        p.memset(PT[:, po:po + z], 0.0, eng="pool")
                    p.act(PT[:, po + z:po + 512], bb[:, z:512], AF.Exp,
                          scale=0.125, bias=B["bcol"][:, hh:hh + 1])
                    p.tt(PT[:, po + z:po + z + 128],
                         PT[:, po + z:po + z + 128], B["triumask"],
                         ALU.mult)
                else:
                    p.act(PT[:, po:po + 512], bb, AF.Exp, scale=0.125,
                          bias=B["bcol"][:, hh:hh + 1])

        def av(kt):
            po = 512 * (kt % 4)
            p.mm(psf[4][0:65, 0:512], B["vtm"][:, 132 * kt:132 * kt + 65],
                 B["PT0"][:, po:po + 512],
                 start=(kt == 0), stop=(kt == nkt - 1))
            p.mm(psf[5][0:65, 0:512],
                 B["vtm"][:, 132 * kt + 66:132 * kt + 131],
                 B["PT1"][:, po:po + 512],
                 start=(kt == 0), stop=(kt == nkt - 1))

        scores(0)
        for kt in range(nkt):
            expmask(kt)
            if kt + 1 < nkt:
                scores(kt + 1)
            av(kt)
        for hh, avb, obuf in ((0, psf[4], "qro"), (1, psf[5], "xab")):
            dn = B["m_row"][0:1, 0:512]
            dnh = B["r_row_hi"][0:1, 0:512]
            dnl = B["r_row_lo"][0:1, 0:512]
            p.recip(dn, avb[64:65, 0:512])
            p.copy(dnh, dn)
            p.tt(dnl, dn, dnh, ALU.subtract)
            p.mm(psf[2][0:64, 0:512], B["ones1"][0:1, 0:64], dnh,
                 start=True, stop=False)
            p.mm(psf[2][0:64, 0:512], B["ones1"][0:1, 0:64], dnl,
                 start=False, stop=True)
            p.copy(B["t1"][0:64, 0:512], avb[0:64, 0:512])
            p.tt(B[obuf][0:64, 0:512], B["t1"][0:64, 0:512],
                 psf[2][0:64, 0:512], ALU.mult)
            p.dma("sp", dram["at_sp"][128 * pr + 64 * hh:
                                      128 * pr + 64 * (hh + 1), qsl],
                  B[obuf][0:64, 0:512])


def emit_attn_qtile_T_old(p, B, psf, dram, S, pr, causal):
    NQ = S // 128
    pcols = slice(128 * pr, 128 * (pr + 1))
    for qi in range(NQ):
        qsl = slice(128 * qi, 128 * (qi + 1))
        nkt = (qi + 1) if causal else NQ
        for kg in range((nkt + 3) // 4):
            sb = kg % 2
            k0 = 4 * kg
            kts = range(k0, min(k0 + 4, nkt))
            for kt in kts:
                ksl = slice(128 * kt, 128 * (kt + 1))
                co = 128 * (kt - k0)
                p.mm(psf[2 * sb][:, co:co + 128],
                     B["kf"][0:64, ksl], B["qf"][0:64, qsl])
                p.mm(psf[2 * sb + 1][:, co:co + 128],
                     B["kf"][64:128, ksl], B["qf"][64:128, qsl])
            w = 128 * (len(list(kts)))
            for hh in range(2):
                p.act(B["PT%d" % hh][:, 128 * k0:128 * k0 + w],
                      psf[2 * sb + hh][:, 0:w], AF.Exp, scale=0.125,
                      bias=B["bcol"][:, hh:hh + 1])
            if causal and nkt - 1 in kts:
                dt_ = slice(128 * (nkt - 1), 128 * nkt)
                for hh in range(2):
                    p.tt(B["PT%d" % hh][:, dt_], B["PT%d" % hh][:, dt_],
                         B["triumask"], ALU.mult)
            for kt in kts:
                ksl = slice(128 * kt, 128 * (kt + 1))
                p.mm(psf[4][0:65, 0:128], B["vtm"][:, 132 * kt:132 * kt + 65],
                     B["PT0"][:, ksl], start=(kt == 0), stop=(kt == nkt - 1))
                p.mm(psf[5][0:65, 0:128],
                     B["vtm"][:, 132 * kt + 66:132 * kt + 131],
                     B["PT1"][:, ksl], start=(kt == 0), stop=(kt == nkt - 1))
        for hh, avb, bcb, obuf in ((0, psf[4], psf[0], "attn_o"),
                                   (1, psf[5], psf[1], "xab")):
            p.recip(B["dnr"], avb[64:65, 0:128])
            p.copy(B["dnrh"], B["dnr"])
            p.tt(B["dnrl"], B["dnr"], B["dnrh"], ALU.subtract)
            p.mm(bcb[0:64, 0:128], B["ones1"][0:1, 0:64], B["dnrh"],
                 start=True, stop=False)
            p.mm(bcb[0:64, 0:128], B["ones1"][0:1, 0:64], B["dnrl"],
                 start=False, stop=True)
            nt = B["numt"][0:64, :] if hh == 0 else B["attn_t"][0:64, :]
            p.copy(nt, avb[0:64, 0:128])
            p.tt(B[obuf][0:64, 0:128], nt, bcb[0:64, 0:128], ALU.mult)
            p.dma("sp", dram["at_sp"][128 * pr + 64 * hh:
                                      128 * pr + 64 * (hh + 1), qsl],
                  B[obuf][0:64, 0:128])


def emit_score_bound(p, B, psf, S):
    for tag, src in ((0, "qf"), (1, "kf")):
        for j in range(S // 512):
            sl = slice(512 * j, 512 * (j + 1))
            p.act(B["sq2"], B[src][:, sl], AF.Square)
            p.mm(psf[0][0:2, :], B["ones2blk"], B["sq2"])
            p.red(B["nrmt"][0:2, j:j + 1], psf[0][0:2, :], ALU.max)
        p.red(B["nrm"][0:2, tag:tag + 1], B["nrmt"][0:2, 0:S // 512],
              ALU.max)
    p.act(B["nrm_s"][0:2, 0:2], B["nrm"][0:2, 0:2], AF.Sqrt)
    p.tt(B["bneg"][0:2, 0:1], B["nrm_s"][0:2, 0:1], B["nrm_s"][0:2, 1:2],
         ALU.mult)
    p.ts(B["bneg"][0:2, 0:1], B["bneg"][0:2, 0:1], -0.125)
    p.transp(psf[1][0:1, 0:2], B["bneg"][0:2, 0:1], B["ident_f"][0:2, 0:2])
    p.copy(B["bnrow"][0:1, 0:2], psf[1][0:1, 0:2])
    p.mm(psf[1][:, 0:2], B["ones1"], B["bnrow"][0:1, 0:2])
    p.copy(B["bcol"][:, 0:2], psf[1][:, 0:2])


def const_arrays():
    ones2blk = np.zeros((128, 2), dtype=BF)
    ones2blk[0:64, 0] = 1
    ones2blk[64:128, 1] = 1
    return {
        "ident_b": np.eye(128, dtype=BF),
        "ident_f": np.eye(128, dtype=np.float32),
        "ones1": np.ones((1, 128), dtype=BF),
        "ones1f": np.ones((1, 128), dtype=np.float32),
        "onescol": np.ones((128, 1), dtype=BF),
        "ones2blk": ones2blk,
    }


def declare_consts(nc):
    d = {}
    d["ident_b"] = nc.dram_tensor("ident_b", [128, 128], BF16,
                                  kind="ExternalInput").ap()
    d["ident_f"] = nc.dram_tensor("ident_f", [128, 128], F32,
                                  kind="ExternalInput").ap()
    d["ones1"] = nc.dram_tensor("ones1", [1, 128], BF16,
                                kind="ExternalInput").ap()
    d["ones1f"] = nc.dram_tensor("ones1f", [1, 128], F32,
                                 kind="ExternalInput").ap()
    d["onescol"] = nc.dram_tensor("onescol", [128, 1], BF16,
                                  kind="ExternalInput").ap()
    d["ones2blk"] = nc.dram_tensor("ones2blk", [128, 2], BF16,
                                   kind="ExternalInput").ap()
    return d


def declare_dram_a(nc, S, x_in_kind="ExternalInput",
                   x_out_kind="ExternalOutput", consts=None):
    d = dict(consts) if consts else declare_consts(nc)
    if "x_in" not in d:
        d["x_in"] = nc.dram_tensor("x_in", [D, S], F32, kind=x_in_kind).ap()
        d["x_out"] = nc.dram_tensor("x_out", [D, S], F32,
                                    kind=x_out_kind).ap()
    for nm in ("wq", "wk", "wv", "wo"):
        d[nm] = nc.dram_tensor("a_" + nm, [D, D], F32,
                               kind="ExternalInput").ap()
    for nm in ("bq_c", "bk_c", "bv_c", "bo_c", "gamma_c", "beta_c"):
        d[nm] = nc.dram_tensor("a_" + nm, [128, NCF], F32,
                               kind="ExternalInput").ap()
    d["bv_r"] = nc.dram_tensor("a_bv_r", [1, D], F32,
                               kind="ExternalInput").ap()
    d["peT"] = nc.dram_tensor("a_peT", [D, S], F32,
                              kind="ExternalInput").ap()
    d["xa_sp"] = nc.dram_tensor("a_xa_sp", [D, S], BF16,
                                kind="Internal").ap()
    d["at_sp"] = nc.dram_tensor("a_at_sp", [D, S], BF16,
                                kind="Internal").ap()
    return d


def _col_layout(b):
    return np.ascontiguousarray(b.reshape(-1, 128).T.astype(np.float32))


def prep_a_inputs(inputs, S):
    g = lambda k: np.asarray(inputs["L0_attn_" + k])
    m = dict(const_arrays())
    m["a_wq"] = np.ascontiguousarray(g("wq"), dtype=np.float32)
    m["a_wk"] = np.ascontiguousarray(g("wk"), dtype=np.float32)
    m["a_wv"] = np.ascontiguousarray(g("wv"), dtype=np.float32)
    m["a_wo"] = np.ascontiguousarray(g("wo"), dtype=np.float32)
    m["a_bq_c"] = _col_layout(g("bq"))
    m["a_bk_c"] = _col_layout(g("bk"))
    m["a_bv_c"] = _col_layout(g("bv"))
    m["a_bo_c"] = _col_layout(g("bo"))
    m["a_gamma_c"] = _col_layout(g("gamma"))
    m["a_beta_c"] = _col_layout(g("beta"))
    m["a_bv_r"] = np.ascontiguousarray(g("bv").reshape(1, D),
                                       dtype=np.float32)
    m["a_peT"] = np.ascontiguousarray(g("pe")[0, :S, :].T,
                                      dtype=np.float32)
    return m


def build_block_a(S):
    nc = bass.Bass("TRN2", target_bir_lowering=False)
    with ExitStack() as ctx:
        dram = declare_dram_a(nc, S)
        B = make_common_bufs(nc, ctx, S)
        psf, psb = make_psum(nc, ctx)
        dram["psf"], dram["psb"] = psf, psb
        p = Prog(nc, ctx)
        load_consts(p, B, dram)
        emit_block_a(p, B, psf, psb, dram, S)
        p.finalize()
    return nc


PAIRSWAP = sum([[2 * i + 1, 2 * i] for i in range(16)], [])


def ln_stats_group(p, B, psf, src, eps):
    for c in range(NCF):
        sl = slice(512 * c, 512 * (c + 1))
        p.copy(B["xcb"], B[src][:, sl])
        p.act(B["sq2"], B["xcb"], AF.Square)
        p.mm(psf[4][0:1, :], B["onescol"], B["xcb"],
             start=(c == 0), stop=(c == NCF - 1))
        p.mm(psf[5][0:1, :], B["onescol"], B["sq2"],
             start=(c == 0), stop=(c == NCF - 1))
    row_stats(p, B, psf[4][0:1, :], psf[5][0:1, :], eps=eps)


def emit_block_b(p, B, psf, psb, dram, S):
    NQ = S // 128
    NKC = S // 512
    NTG = S // 512
    ps8 = [psf[0], psf[1], psf[2], psf[3], psf[4], psf[5],
           psb[0][:].bitcast(F32)[:, 0:512], psb[1][:].bitcast(F32)[:, 0:512]]

    p.begin_params()
    for nm in ("bq_c", "bk_c", "bo_c", "ln1w_c", "ln1b_c", "ln2w_c",
               "ln2b_c"):
        p.dma("sp", B["pc_" + nm], dram[nm])
    p.dma("sp", B["bv_row"], dram["bv_r"])
    p.dma("sp", B["gbrow"], dram["gateb_r"])
    p.end_params()
    p.begin_params()
    p.dma("sp", B["triumask"], dram["triumaskf"])
    p.dma("pool", B["expb8"], dram["exp_b"])
    p.end_params()
    init_vtm_ones(p, B, S)
    p.copy(B["bvr_hi"], B["bv_row"])
    p.tt(B["bvr_lo"], B["bv_row"], B["bvr_hi"], ALU.subtract)
    p.copy(B["gbh"], B["gbrow"])
    p.tt(B["gbl"], B["gbrow"], B["gbh"], ALU.subtract)

    for g in range(NTG):
        cols = slice(512 * g, 512 * (g + 1))
        load_group(p, B, dram["x_in"], cols)
        ln_stats_group(p, B, psf, "xg", eps=1e-5)
        bcast_rows(p, B, psf[2][:, :], B["r_row_hi"][0:1, :],
                   B["r_row_lo"][0:1, :])
        bcast_rows(p, B, psf[3][:, :], B["mr_row_hi"][0:1, :],
                   B["mr_row_lo"][0:1, :])
        p.dma("sp", B["ropeA_t"], dram["ropeA"][:, cols])
        p.dma("sp", B["ropeB_t"], dram["ropeB"][:, cols])
        for c in range(NCF):
            sl = slice(512 * c, 512 * (c + 1))
            rows = slice(128 * c, 128 * (c + 1))
            p.tt(B["t1"], B["xg"][:, sl], psf[2][:, :], ALU.mult)
            p.tt(B["xab"], B["t1"], psf[3][:, :], ALU.subtract)
            p.dma("sp", dram["bin_sp"][rows, cols], B["xab"])
            p.shuffle(B["swp"], B["xab"], PAIRSWAP)
            p.tt(B["t2"], B["xab"], B["ropeA_t"], ALU.mult)
            p.tt(B["t3"], B["swp"], B["ropeB_t"], ALU.mult)
            p.tt(B["qro"], B["t2"], B["t3"], ALU.add)
            p.dma("sp", dram["qin_sp"][rows, cols], B["qro"])
        cv_pump(8)
    p.fence()

    for pr in range(NPAIR):
        pcols = slice(128 * pr, 128 * (pr + 1))
        for wi, wn in enumerate(("wqp", "wkp", "wvp")):
            load_2d(p, B[wn][:, 0:D],
                    dram["in_wT"][:, wi * D + 128 * pr:
                                  wi * D + 128 * (pr + 1)], "pool")
        for j in range(S // 512):
            jc = slice(512 * j, 512 * (j + 1))
            load_2d(p, B["xa_tA"][:, 0:NCF * 512], dram["qin_sp"][:, jc],
                    "act")
            load_2d(p, B["xa_tB"][:, 0:NCF * 512], dram["bin_sp"][:, jc],
                    "sp")
            for c in range(NCF):
                xac = B["xa_tA"][:, 512 * c:512 * (c + 1)]
                p.mm(psf[0], B["wqp"][:, 128 * c:128 * (c + 1)], xac,
                     start=(c == 0), stop=(c == NCF - 1))
                p.mm(psf[1], B["wkp"][:, 128 * c:128 * (c + 1)], xac,
                     start=(c == 0), stop=(c == NCF - 1))
            p.ts(B["qf"][:, jc], psf[0][:, :], B["pc_bq_c"][:, pr:pr + 1],
                 op0=ALU.add)
            p.ts(B["kf"][:, jc], psf[1][:, :], B["pc_bk_c"][:, pr:pr + 1],
                 op0=ALU.add)
            for tt_ in range(4):
                kt = 4 * j + tt_
                for c in range(NCF):
                    p.mm(psf[2][:, 0:128],
                         B["xa_tB"][:, 512 * c + 128 * tt_:
                                    512 * c + 128 * (tt_ + 1)],
                         B["wvp"][:, 128 * c:128 * (c + 1)],
                         start=(c == 0), stop=False)
                p.mm(psf[2][:, 0:128], B["ones1"],
                     B["bvr_hi"][0:1, pcols], start=False, stop=False)
                p.mm(psf[2][:, 0:128], B["ones1"],
                     B["bvr_lo"][0:1, pcols], start=False, stop=True)
                p.copy(B["vtm"][:, 132 * kt:132 * kt + 64], psf[2][:, 0:64])
                p.copy(B["vtm"][:, 132 * kt + 66:132 * kt + 130],
                       psf[2][:, 64:128])
        emit_score_bound(p, B, psf, S)
        emit_attn_qtile_T(p, B, psf, psb, dram, S, pr, causal=True)
        cv_pump(8)
    p.fence()

    load_2d(p, B["wob"][:, 0:NCF * D], dram["op_wT"][:, :], "pool")
    p.begin_params()
    for c in range(NCF):
        p.dma("sp", B["gw_s"][:, 8 * c:8 * (c + 1)],
              dram["gw_T"][128 * c:128 * (c + 1), :])
    p.end_params()
    for j in range(S // 512):
        jc = slice(512 * j, 512 * (j + 1))
        atb = B["at_t"] if j % 2 == 0 else B["qf"]
        xbb = B["xa_tB"] if j % 2 == 0 else B["kf"]
        load_2d(p, atb[:, 0:NCF * 512], dram["at_sp"][:, jc], "act")
        load_2d(p, xbb[:, 0:NCF * 512], dram["bin_sp"][:, jc], "sp")
        for oc in range(NCF):
            bank = psf[oc % 4]
            for c in range(NCF):
                p.mm(bank,
                     B["wob"][:, D * c + 128 * oc:D * c + 128 * (oc + 1)],
                     atb[:, 512 * c:512 * (c + 1)],
                     start=(c == 0), stop=(c == NCF - 1))
            p.ts(B["t1"], bank[:, :], B["pc_bo_c"][:, oc:oc + 1],
                 op0=ALU.add)
            p.tt(B["hbuf"][:, 512 * oc:512 * (oc + 1)], B["t1"],
                 xbb[:, 512 * oc:512 * (oc + 1)], ALU.add)
        ln_stats_group(p, B, psf, "hbuf", eps=1e-5)
        bcast_rows(p, B, psf[4][:, :], B["r_row_hi"][0:1, :],
                   B["r_row_lo"][0:1, :])
        bcast_rows(p, B, psf[5][:, :], B["mr_row_hi"][0:1, :],
                   B["mr_row_lo"][0:1, :])
        for oc in range(NCF):
            sl = slice(512 * oc, 512 * (oc + 1))
            rows = slice(128 * oc, 128 * (oc + 1))
            p.tt(B["t1"], B["hbuf"][:, sl], psf[4][:, :], ALU.mult)
            p.tt(B["t2"], B["t1"], psf[5][:, :], ALU.subtract)
            p.ts(B["hbuf"][:, sl], B["t2"], B["pc_ln1w_c"][:, oc:oc + 1],
                 B["pc_ln1b_c"][:, oc:oc + 1], op0=ALU.mult, op1=ALU.add)
            p.copy(B["xa_tA"][:, sl], B["hbuf"][:, sl])
            p.dma("sp", dram["x1q_sp"][rows, jc], B["xa_tA"][:, sl])
        for tt_ in range(4):
            for c in range(NCF):
                p.mm(psf[0][:, 0:8],
                     B["hbuf"][:, 512 * c + 128 * tt_:
                               512 * c + 128 * (tt_ + 1)],
                     B["gw_s"][:, 8 * c:8 * (c + 1)],
                     start=(c == 0), stop=False)
            p.mm(psf[0][:, 0:8], B["ones1f"], B["gbrow"][0:1, 0:8],
                 start=False, stop=True)
            p.copy(B["g8"][:, 0:8], psf[0][:, 0:8])
            p.red(B["gng"][:, 0:1], B["g8"][:, 0:8], ALU.max, negate=True)
            p.act(B["g8e"][:, 0:8], B["g8"][:, 0:8], AF.Exp,
                  bias=B["gng"][:, 0:1], accum=B["gs1"][:, 0:1])
            p.recip(B["gr1"][:, 0:1], B["gs1"][:, 0:1])
            p.ts(B["g8"][:, 0:8], B["g8e"][:, 0:8], B["gr1"][:, 0:1],
                 op0=ALU.mult)
            p.red(B["m1c"][:, 0:1], B["g8"][:, 0:8], ALU.max)
            p.ts(B["gk"][:, 0:8], B["g8"][:, 0:8], B["m1c"][:, 0:1],
                 op0=ALU.is_ge)
            p.stt(B["g8b"][:, 0:8], B["gk"][:, 0:8], -1e9, B["g8"][:, 0:8],
                  ALU.mult, ALU.add)
            p.red(B["m2c"][:, 0:1], B["g8b"][:, 0:8], ALU.max)
            p.ts(B["gk"][:, 0:8], B["g8"][:, 0:8], B["m2c"][:, 0:1],
                 op0=ALU.is_ge)
            p.tt(B["g8"][:, 0:8], B["g8"][:, 0:8], B["gk"][:, 0:8],
                 ALU.mult)
            p.transp(psf[1][0:8, 0:128], B["g8"][:, 0:8], B["ident_f"])
            p.copy(B["cTx"][0:8, 128 * tt_:128 * (tt_ + 1)],
                   psf[1][0:8, 0:128])
        p.dma("sp", dram["c_dram"][:, jc], B["cTx"][0:8, :])
        cv_pump(8)
    p.fence()

    xs_bufs = ["P0", "P1", "PT0", "PT1", "qf", "kf", "vtm", "at_t"]
    for j in range(S // 512):
        jc = slice(512 * j, 512 * (j + 1))
        load_2d(p, B["xa_tA"][:, 0:NCF * 512], dram["x1q_sp"][:, jc],
                "act")
        p.dma("pool", B["cT8b"], dram["c_dram"][:, jc])
        for e in range(8):
            p.dma("sp", B["crow"], dram["c_dram"][e:e + 1, jc])
            p.copy(B["crh"], B["crow"])
            p.tt(B["crl"], B["crow"], B["crh"], ALU.subtract)
            cb = psf[e % 2]
            bcast_rows(p, B, cb[:, :], B["crh"][0:1, :], B["crl"][0:1, :])
            for c in range(NCF):
                sl = slice(512 * c, 512 * (c + 1))
                p.tt(B[xs_bufs[e]][:, sl], B["xa_tA"][:, sl], cb[:, :],
                     ALU.mult)
        for e in range(8):
            wbuf = B["wob"]
            if CV is not None:
                load_2d(p, wbuf[:, 0:NCF * D], dram["exp_bf"][e],
                        "sp" if e % 2 == 0 else "act")
            else:
                load_2d(p, wbuf[:, 0:NCF * D], dram["exp_wT"][e], "pool")
            for oc in range(NCF):
                bank = ps8[oc]
                for c in range(NCF):
                    p.mm(bank,
                         wbuf[:, D * c + 128 * oc:D * c + 128 * (oc + 1)],
                         B[xs_bufs[e]][:, 512 * c:512 * (c + 1)],
                         start=(e == 0 and c == 0), stop=False)
                if e == 7:
                    p.mm(bank, B["expb8"][:, 128 * oc:128 * (oc + 1)],
                         B["cT8b"][0:8, :], start=False, stop=True)
        for oc in range(NCF):
            p.tt(B["hbuf"][:, 512 * oc:512 * (oc + 1)], ps8[oc][:, :],
                 B["xa_tA"][:, 512 * oc:512 * (oc + 1)], ALU.add)
        ln_stats_group(p, B, psf, "hbuf", eps=1e-5)
        bcast_rows(p, B, psf[4][:, :], B["r_row_hi"][0:1, :],
                   B["r_row_lo"][0:1, :])
        bcast_rows(p, B, psf[5][:, :], B["mr_row_hi"][0:1, :],
                   B["mr_row_lo"][0:1, :])
        load_group(p, B, dram["x_in"], jc)
        for oc in range(NCF):
            sl = slice(512 * oc, 512 * (oc + 1))
            rows = slice(128 * oc, 128 * (oc + 1))
            p.tt(B["t1"], B["hbuf"][:, sl], psf[4][:, :], ALU.mult)
            p.tt(B["t2"], B["t1"], psf[5][:, :], ALU.subtract)
            p.ts(B["t3"], B["t2"], B["pc_ln2w_c"][:, oc:oc + 1],
                 B["pc_ln2b_c"][:, oc:oc + 1], op0=ALU.mult, op1=ALU.add)
            p.tt(B["x1o"], B["t3"], B["xg"][:, sl], ALU.add)
            p.dma("sp", dram["x_out"][rows, jc], B["x1o"])
    p.fence()


def declare_dram_b(nc, S, x_in_kind="ExternalInput",
                   x_out_kind="ExternalOutput", consts=None):
    d = dict(consts) if consts else declare_consts(nc)
    if "x_in" not in d:
        d["x_in"] = nc.dram_tensor("x_in", [D, S], F32, kind=x_in_kind).ap()
        d["x_out"] = nc.dram_tensor("x_out", [D, S], F32,
                                    kind=x_out_kind).ap()
    d["in_wT"] = nc.dram_tensor("b_in_wT", [D, 3 * D], F32,
                                kind="ExternalInput").ap()
    d["op_wT"] = nc.dram_tensor("b_op_wT", [D, D], F32,
                                kind="ExternalInput").ap()
    d["exp_wT"] = nc.dram_tensor("b_exp_wT", [8, D, D], F32,
                                 kind="ExternalInput").ap()
    d["exp_b"] = nc.dram_tensor("b_exp_b", [8, D], F32,
                                kind="ExternalInput").ap()
    d["gw_T"] = nc.dram_tensor("b_gw_T", [D, 8], F32,
                               kind="ExternalInput").ap()
    d["gateb_r"] = nc.dram_tensor("b_gateb_r", [1, 8], F32,
                                  kind="ExternalInput").ap()
    for nm in ("bq_c", "bk_c", "bo_c", "ln1w_c", "ln1b_c", "ln2w_c",
               "ln2b_c"):
        d[nm] = nc.dram_tensor("b_" + nm, [128, NCF], F32,
                               kind="ExternalInput").ap()
    d["bv_r"] = nc.dram_tensor("b_bv_r", [1, D], F32,
                               kind="ExternalInput").ap()
    d["triumaskf"] = nc.dram_tensor("b_triuf", [128, 128], F32,
                                    kind="ExternalInput").ap()
    d["ropeA"] = nc.dram_tensor("b_ropeA", [128, S], F32,
                                kind="ExternalInput").ap()
    d["ropeB"] = nc.dram_tensor("b_ropeB", [128, S], F32,
                                kind="ExternalInput").ap()
    d["exp_bf"] = nc.dram_tensor("b_exp_bf", [8, D, D], BF16,
                                 kind="Internal").ap()
    d["bin_sp"] = nc.dram_tensor("b_bin_sp", [D, S], BF16,
                                 kind="Internal").ap()
    d["qin_sp"] = nc.dram_tensor("b_qin_sp", [D, S], BF16,
                                 kind="Internal").ap()
    d["x1q_sp"] = nc.dram_tensor("b_x1q_sp", [D, S], BF16,
                                 kind="Internal").ap()
    d["at_sp"] = nc.dram_tensor("b_at_sp", [D, S], BF16,
                                kind="Internal").ap()
    d["c_dram"] = nc.dram_tensor("b_c_dram", [8, S], F32,
                                 kind="Internal").ap()
    return d


def rope_tables(S):
    inv = (1.0 / (10000.0 ** (np.arange(0, 64, 2, dtype=np.float32)
                              / np.float32(64.0)))).astype(np.float32)
    tpos = np.arange(S, dtype=np.float32)
    freqs = tpos[:, None] * inv[None, :]
    sin = np.sin(freqs).astype(np.float32)
    cos = np.cos(freqs).astype(np.float32)
    A = np.zeros((128, S), dtype=np.float32)
    Bt = np.zeros((128, S), dtype=np.float32)
    for h in range(2):
        for jj in range(64):
            i = jj // 2
            r = 64 * h + jj
            A[r] = cos[:, i]
            Bt[r] = -sin[:, i] if jj % 2 == 0 else sin[:, i]
    return A, Bt


def prep_b_inputs(inputs, S):
    g = lambda k: np.asarray(inputs["L0_ffn_" + k])
    m = {}
    ipw = g("in_proj_w")
    ipb = g("in_proj_b")
    m["b_in_wT"] = np.ascontiguousarray(ipw.T, dtype=np.float32)
    m["b_op_wT"] = np.ascontiguousarray(g("out_proj_w").T, dtype=np.float32)
    m["b_exp_wT"] = np.ascontiguousarray(
        g("expert_w").transpose(0, 2, 1), dtype=np.float32)
    m["b_exp_b"] = np.ascontiguousarray(g("expert_b"), dtype=np.float32)
    m["b_gw_T"] = np.ascontiguousarray(g("gate_w").T, dtype=np.float32)
    m["b_gateb_r"] = np.ascontiguousarray(g("gate_b").reshape(1, 8),
                                          dtype=np.float32)
    m["b_bq_c"] = _col_layout(ipb[0:D])
    m["b_bk_c"] = _col_layout(ipb[D:2 * D])
    m["b_bv_r"] = np.ascontiguousarray(ipb[2 * D:3 * D].reshape(1, D),
                                       dtype=np.float32)
    m["b_bo_c"] = _col_layout(g("out_proj_b"))
    m["b_ln1w_c"] = _col_layout(g("ln1_w"))
    m["b_ln1b_c"] = _col_layout(g("ln1_b"))
    m["b_ln2w_c"] = _col_layout(g("ln2_w"))
    m["b_ln2b_c"] = _col_layout(g("ln2_b"))
    m["b_triuf"] = np.triu(np.ones((128, 128))).astype(np.float32)
    A, Bt = rope_tables(S)
    m["b_ropeA"] = A
    m["b_ropeB"] = Bt
    return m


def build_block_b(S):
    nc = bass.Bass("TRN2", target_bir_lowering=False)
    with ExitStack() as ctx:
        dram = declare_dram_b(nc, S)
        B = make_common_bufs(nc, ctx, S)
        B.add("gw_s", [128, 8 * NCF], F32)
        for nm, sh, dt_ in (("g8e", [128, 8], F32), ("gs1", [128, 1], F32),
                            ("gr1", [128, 1], F32), ("gng", [128, 1], F32)):
            B.add(nm, sh, dt_)
        psf, psb = make_psum(nc, ctx)
        p = Prog(nc, ctx)
        load_consts(p, B, dram)
        emit_block_b(p, B, psf, psb, dram, S)
        p.finalize()
    return nc


def add_c_bufs(B):
    B.t["phqg"] = B.t["qf"]
    B.t["phkg"] = B.t["kf"]
    B.t["pvtg"] = B.t["vtm"]
    B.t["h1g"] = B.t["P0"]
    B.t["w2s"] = B.t["P1"]
    B.add("kvst", [128, 66 * NPAIR], BF16)
    B.add("kvstf", [128, 64 * NPAIR], F32)
    B.add("phkT", [128, 128], BF16)
    B.add("Aa", [128, 128], BF16)
    B.t["Ab"] = B.t["attn_o"]
    B.add("triumask", [128, 128], F32)
    B.add("dnr", [1, 128], F32)
    B.add("dnrh", [1, 128], BF16)
    B.add("dnrl", [1, 128], BF16)
    B.t["numt"] = B.t["attn_t"]
    B.add("sstf", [128, NPAIR], F32)
    for nm in ("r1_row", "mr1_row"):
        B.add(nm, [1, 512], F32)
        B.add(nm + "_hi", [1, 512], BF16)
        B.add(nm + "_lo", [1, 512], BF16)
    B.add("u_row", [1, 512], F32)
    for nm in ("ln1s_c", "ln1bc_c", "ln2s_c", "ln2bc_c", "b2_c"):
        B.add("pc_" + nm, [128, NCF], F32)
    B.add("b1_c", [128, 32], F32)


def row_stats_c(p, B, sum_ap, sumsq_ap):
    m = B["m_row"][0:1, :]
    q = B["q_row"][0:1, :]
    t = B["t_row"][0:1, :]
    r1 = B["r1_row"][0:1, :]
    mr1 = B["mr1_row"][0:1, :]
    u = B["u_row"][0:1, :]
    R = B["r_row"][0:1, :]
    mR = B["mr_row"][0:1, :]
    p.ts(m, sum_ap, 1.0 / D)
    p.ts(q, sumsq_ap, 1.0 / D)
    p.tt(t, m, m, ALU.mult)
    p.stt(q, q, 1e-5, t, ALU.add, ALU.subtract)
    p.act(t, q, AF.Sqrt)
    p.recip(r1, t)
    p.tt(mr1, m, r1, ALU.mult)
    p.ts(u, q, -1e-5, op0=ALU.add)
    p.tt(u, u, r1, ALU.mult)
    p.tt(u, u, r1, ALU.mult)
    p.ts(u, u, 1e-6, op0=ALU.add)
    p.act(t, u, AF.Sqrt)
    p.recip(u, t)
    p.tt(R, r1, u, ALU.mult)
    p.tt(mR, m, R, ALU.mult)
    for nm in ("r_row", "mr_row", "r1_row", "mr1_row"):
        s_ = B[nm][0:1, :]
        p.copy(B[nm + "_hi"][0:1, :], s_)
        p.tt(B[nm + "_lo"][0:1, :], s_, B[nm + "_hi"][0:1, :], ALU.subtract)


def emit_block_c(p, B, psf, psb, dram, S):
    NTG = S // 512
    psb0f = psb[0][:].bitcast(F32)
    psb1f = psb[1][:].bitcast(F32)
    psf2b = psf[2][:].bitcast(BF16)
    wo_b = ("PT0", "PT1")

    def w_ap(bufs, c, ocols):
        buf = B[bufs[0]] if c < 4 else B[bufs[1]]
        cc = c % 4
        return buf[:, 1024 * cc + ocols.start:1024 * cc + ocols.stop]

    p.begin_params()
    for nm in ("ln1s_c", "ln1bc_c", "ln2s_c", "ln2bc_c", "b2_c"):
        p.dma("sp", B["pc_" + nm], dram[nm])
    p.dma("sp", B["b1_c"], dram["b1_c"])
    p.dma("sp", B["triumask"], dram["triumask"])
    p.end_params()
    p.begin_params()
    for c in range(NCF):
        buf = B[wo_b[0]] if c < 4 else B[wo_b[1]]
        p.dma("pool", buf[:, 1024 * (c % 4):1024 * (c % 4 + 1)],
              dram["wo"][128 * c:128 * (c + 1), :])
    p.end_params()
    p.memset(B["kvst"][:], 0.0, eng="dve")
    p.memset(B["kvstf"][:], 0.0, eng="dve")
    p.memset(B["sstf"][:], 0.0, eng="dve")

    for g in range(NTG):
        cols = slice(512 * g, 512 * (g + 1))
        load_group(p, B, dram["x_in"], cols)
        for c in range(NCF):
            sl = slice(512 * c, 512 * (c + 1))
            p.copy(B["xcb"], B["xg"][:, sl])
            p.act(B["sq2"], B["xcb"], AF.Square)
            p.mm(psf[4][0:1, :], B["onescol"], B["xcb"],
                 start=(c == 0), stop=(c == NCF - 1))
            p.mm(psf[5][0:1, :], B["onescol"], B["sq2"],
                 start=(c == 0), stop=(c == NCF - 1))
        row_stats_c(p, B, psf[4][0:1, :], psf[5][0:1, :])
        bcast_rows(p, B, psf[2][:, :], B["r_row_hi"][0:1, :],
                   B["r_row_lo"][0:1, :])
        bcast_rows(p, B, psf[3][:, :], B["mr_row_hi"][0:1, :],
                   B["mr_row_lo"][0:1, :])
        for c in range(NCF):
            sl = slice(512 * c, 512 * (c + 1))
            p.tt(B["t1"], B["xg"][:, sl], psf[2][:, :], ALU.mult)
            p.tt(B["t2"], B["t1"], psf[3][:, :], ALU.subtract)
            p.ts(B["xa_tA"][:, sl], B["t2"], B["pc_ln1s_c"][:, c:c + 1],
                 B["pc_ln1bc_c"][:, c:c + 1], op0=ALU.mult, op1=ALU.add)
        for pr in range(NPAIR):
            pcols = slice(128 * pr, 128 * (pr + 1))
            if CV is not None and pr % 2 == 1:
                wq_b = B["ropeA_t"][:].bitcast(BF16)
                wk_b = B["ropeB_t"][:].bitcast(BF16)
                wv_b = B["t2"][:].bitcast(BF16)
            else:
                wq_b = B["wqp"][:]
                wk_b = B["wkp"][:]
                wv_b = B["wvp"][:]
            if CV is not None:
                for wn, buf, q in (("wq", wq_b, "sp"), ("wk", wk_b, "act"),
                                   ("wv", wv_b, "sp")):
                    p.dma(q, buf[:, 0:D],
                          dram[wn + "_bf"][:, pcols].rearrange(
                              "(t p) n -> p t n", p=128))
            else:
                for c in range(NCF):
                    rows = slice(128 * c, 128 * (c + 1))
                    for wn, buf in (("wq", wq_b), ("wk", wk_b),
                                    ("wv", wv_b)):
                        p.dma("pool", buf[:, 128 * c:128 * (c + 1)],
                              dram[wn][rows, pcols])
            for c in range(NCF):
                p.mm(psf[0], wq_b[:, 128 * c:128 * (c + 1)],
                     B["xa_tA"][:, 512 * c:512 * (c + 1)],
                     start=(c == 0), stop=(c == NCF - 1))
                p.mm(psf[1], wk_b[:, 128 * c:128 * (c + 1)],
                     B["xa_tA"][:, 512 * c:512 * (c + 1)],
                     start=(c == 0), stop=(c == NCF - 1))
            p.ts(B["phqg"][:, 512 * pr:512 * (pr + 1)], psf[0][:, :],
                 0.0, 0.001, op0=ALU.max, op1=ALU.add)
            p.ts(B["phkg"][:, 512 * pr:512 * (pr + 1)], psf[1][:, :],
                 0.0, 0.001, op0=ALU.max, op1=ALU.add)
            for tt_ in range(4):
                for c in range(NCF):
                    p.mm(psf[3][:, 0:128],
                         B["xa_tA"][:, 512 * c + 128 * tt_:
                                    512 * c + 128 * (tt_ + 1)],
                         wv_b[:, 128 * c:128 * (c + 1)],
                         start=(c == 0), stop=(c == NCF - 1))
                p.copy(B["pvtg"][:, 512 * pr + 128 * tt_:
                                 512 * pr + 128 * (tt_ + 1)],
                       psf[3][:, 0:128])
        for tt_ in range(4):
            for pr in range(NPAIR):
                qsl = slice(512 * pr + 128 * tt_,
                            512 * pr + 128 * (tt_ + 1))
                kvo = 66 * pr
                p.mm(psf[0][:, 0:128], B["phkg"][0:64, qsl],
                     B["phqg"][0:64, qsl])
                p.mm(psf[1][:, 0:128], B["phkg"][64:128, qsl],
                     B["phqg"][64:128, qsl])
                p.tt(B["Aa"], psf[0][:, 0:128], B["triumask"], ALU.mult)
                p.tt(B["Ab"], psf[1][:, 0:128], B["triumask"], ALU.mult)
                p.mm(psf[4][0:64, 0:128],
                     B["pvtg"][:, qsl.start:qsl.start + 64],
                     B["Aa"], start=True, stop=False, tile_position=(0, 0))
                p.mm(psf[4][0:64, 0:128], B["kvst"][0:64, kvo:kvo + 64],
                     B["phqg"][0:64, qsl], start=False, stop=True)
                p.mm(psf[5][64:128, 0:128],
                     B["pvtg"][:, qsl.start + 64:qsl.start + 128],
                     B["Ab"], start=True, stop=False, tile_position=(0, 64))
                p.mm(psf[5][64:128, 0:128], B["kvst"][64:128, kvo:kvo + 64],
                     B["phqg"][64:128, qsl], start=False, stop=True,
                     tile_position=(64, 64))
                p.mm(psf[3][0:1, 128:256], B["onescol"], B["Aa"],
                     start=True, stop=False)
                p.mm(psf[3][0:1, 128:256], B["kvst"][0:64, kvo + 64:kvo + 65],
                     B["phqg"][0:64, qsl], start=False, stop=True)
                p.mm(psf[3][0:1, 256:384], B["onescol"], B["Ab"],
                     start=True, stop=False)
                p.mm(psf[3][0:1, 256:384],
                     B["kvst"][64:128, kvo + 64:kvo + 65],
                     B["phqg"][64:128, qsl], start=False, stop=True)
                p.transp(psf2b[:, 0:128], B["phkg"][:, qsl], B["ident_b"])
                p.copy(B["phkT"], psf2b[:, 0:128])
                kvc = slice(64 * pr, 64 * (pr + 1))
                p.mm(psb0f[0:64, kvc], B["ident_f"][0:64, 0:64],
                     B["kvstf"][0:64, kvc], start=True, stop=False,
                     tile_position=(0, 0))
                p.mm(psb0f[0:64, kvc], B["phkT"][:, 0:64],
                     B["pvtg"][:, qsl.start:qsl.start + 64],
                     start=False, stop=True, tile_position=(0, 0))
                p.mm(psb1f[64:128, kvc], B["ident_f"][64:128, 64:128],
                     B["kvstf"][64:128, kvc], start=True, stop=False,
                     tile_position=(64, 64))
                p.mm(psb1f[64:128, kvc], B["phkT"][:, 64:128],
                     B["pvtg"][:, qsl.start + 64:qsl.start + 128],
                     start=False, stop=True, tile_position=(0, 64))
                p.mm(psf[3][:, 1:2], B["phkT"], B["onescol"].bitcast(BF16)
                     if False else B["onescol"], start=True, stop=True)
                p.tt(B["sstf"][:, pr:pr + 1], B["sstf"][:, pr:pr + 1],
                     psf[3][:, 1:2], ALU.add)
                p.copy(B["kvstf"][0:64, kvc], psb0f[0:64, kvc])
                p.copy(B["kvstf"][64:128, kvc], psb1f[64:128, kvc])
                p.copy(B["kvst"][0:64, kvo:kvo + 64], psb0f[0:64, kvc])
                p.copy(B["kvst"][64:128, kvo:kvo + 64],
                       psb1f[64:128, kvc])
                p.copy(B["kvst"][0:64, kvo + 64:kvo + 65],
                       B["sstf"][0:64, pr:pr + 1])
                p.copy(B["kvst"][64:128, kvo + 64:kvo + 65],
                       B["sstf"][64:128, pr:pr + 1])
                p.recip(B["dnr"], psf[3][0:1, 128:256])
                p.copy(B["dnrh"], B["dnr"])
                p.tt(B["dnrl"], B["dnr"], B["dnrh"], ALU.subtract)
                p.mm(psf[0][0:64, 0:128], B["ones1"][0:1, 0:64],
                     B["dnrh"], start=True, stop=False, tile_position=(0, 0))
                p.mm(psf[0][0:64, 0:128], B["ones1"][0:1, 0:64],
                     B["dnrl"], start=False, stop=True, tile_position=(0, 0))
                p.recip(B["dnr"], psf[3][0:1, 256:384])
                p.copy(B["dnrh"], B["dnr"])
                p.tt(B["dnrl"], B["dnr"], B["dnrh"], ALU.subtract)
                p.mm(psf[1][64:128, 0:128], B["ones1"][0:1, 0:64],
                     B["dnrh"], start=True, stop=False,
                     tile_position=(0, 64))
                p.mm(psf[1][64:128, 0:128], B["ones1"][0:1, 0:64],
                     B["dnrl"], start=False, stop=True,
                     tile_position=(0, 64))
                p.copy(B["numt"][0:64, :], psf[4][0:64, 0:128])
                p.copy(B["numt"][64:128, :], psf[5][64:128, 0:128])
                p.tt(B["phqg"][0:64, qsl], B["numt"][0:64, :],
                     psf[0][0:64, 0:128], ALU.mult)
                p.tt(B["phqg"][64:128, qsl], B["numt"][64:128, :],
                     psf[1][64:128, 0:128], ALU.mult)
        bcast_rows(p, B, psf[2][:, :], B["r1_row_hi"][0:1, :],
                   B["r1_row_lo"][0:1, :])
        bcast_rows(p, B, psf[3][:, :], B["mr1_row_hi"][0:1, :],
                   B["mr1_row_lo"][0:1, :])
        for oc in range(NCF):
            ocols = slice(128 * oc, 128 * (oc + 1))
            bank = psf[oc % 2]
            for c in range(NCF):
                p.mm(bank, w_ap(wo_b, c, ocols),
                     B["phqg"][:, 512 * c:512 * (c + 1)],
                     start=(c == 0), stop=(c == NCF - 1))
            sl = slice(512 * oc, 512 * (oc + 1))
            p.tt(B["t1"], B["xg"][:, sl], psf[2][:, :], ALU.mult)
            p.tt(B["t2"], B["t1"], psf[3][:, :], ALU.subtract)
            p.tt(B["hbuf"][:, sl], B["t2"], bank[:, :], ALU.add)
            p.copy(B["xa_tB"][:, sl], B["hbuf"][:, sl])
        ln_stats_group(p, B, psf, "hbuf", eps=1e-6)
        bcast_rows(p, B, psf[4][:, :], B["r_row_hi"][0:1, :],
                   B["r_row_lo"][0:1, :])
        bcast_rows(p, B, psf[5][:, :], B["mr_row_hi"][0:1, :],
                   B["mr_row_lo"][0:1, :])
        for oc in range(NCF):
            sl = slice(512 * oc, 512 * (oc + 1))
            p.tt(B["t1"], B["hbuf"][:, sl], psf[4][:, :], ALU.mult)
            p.tt(B["t2"], B["t1"], psf[5][:, :], ALU.subtract)
            p.ts(B["phkg"][:, sl], B["t2"], B["pc_ln2s_c"][:, oc:oc + 1],
                 B["pc_ln2bc_c"][:, oc:oc + 1], op0=ALU.mult, op1=ALU.add)
        for b in range(4):
            if CV is not None:
                load_2d(p, B["wob"][:, 0:NCF * D],
                        dram["w1_bf"][:, 1024 * b:1024 * (b + 1)],
                        "sp" if b % 2 == 0 else "act")
            else:
                load_2d(p, B["wob"][:, 0:NCF * D],
                        dram["w1"][:, 1024 * b:1024 * (b + 1)], "pool")
            for hc in range(8):
                bank = psf[hc % 2]
                for c in range(NCF):
                    p.mm(bank,
                         B["wob"][:, 1024 * c + 128 * hc:
                                  1024 * c + 128 * (hc + 1)],
                         B["phkg"][:, 512 * c:512 * (c + 1)],
                         start=(c == 0), stop=(c == NCF - 1))
                p.act(B["h1g"][:, 512 * hc:512 * (hc + 1)], bank[:, :],
                      AF.Gelu_apprx_tanh,
                      bias=B["b1_c"][:, 8 * b + hc:8 * b + hc + 1])
            for oh in range(2):
                if CV is not None:
                    load_2d(p, B["w2s"][:, 0:8 * 512],
                            dram["w2_bf"][1024 * b:1024 * (b + 1),
                                          512 * oh:512 * (oh + 1)],
                            "act" if b % 2 == 0 else "sp")
                else:
                    load_2d(p, B["w2s"][:, 0:8 * 512],
                            dram["w2"][1024 * b:1024 * (b + 1),
                                       512 * oh:512 * (oh + 1)], "pool")
                for oc4 in range(4):
                    oc = 4 * oh + oc4
                    bank = psf[2 + oc4 % 2]
                    for hc in range(8):
                        p.mm(bank,
                             B["w2s"][:, 512 * hc + 128 * oc4:
                                      512 * hc + 128 * (oc4 + 1)],
                             B["h1g"][:, 512 * hc:512 * (hc + 1)],
                             start=(hc == 0), stop=(hc == 7))
                    sl = slice(512 * oc, 512 * (oc + 1))
                    if b == 0:
                        p.tt(B["hbuf"][:, sl], bank[:, :], B["xa_tB"][:, sl],
                             ALU.mult) if False else None
                        p.copy(B["hbuf"][:, sl], bank[:, :])
                    else:
                        p.tt(B["hbuf"][:, sl], B["hbuf"][:, sl], bank[:, :],
                             ALU.add)
        for oc in range(NCF):
            sl = slice(512 * oc, 512 * (oc + 1))
            rows = slice(128 * oc, 128 * (oc + 1))
            p.ts(B["t1"], B["hbuf"][:, sl], B["pc_b2_c"][:, oc:oc + 1],
                 op0=ALU.add)
            p.tt(B["t2"], B["t1"], B["xa_tB"][:, sl], ALU.add)
            p.tt(B["x1o"], B["t2"], B["xg"][:, sl], ALU.add)
            p.dma("sp", dram["x_out"][rows, cols], B["x1o"])
    p.fence()


def declare_dram_c(nc, S, x_in_kind="ExternalInput",
                   x_out_kind="ExternalOutput", consts=None):
    d = dict(consts) if consts else declare_consts(nc)
    if "x_in" not in d:
        d["x_in"] = nc.dram_tensor("x_in", [D, S], F32, kind=x_in_kind).ap()
        d["x_out"] = nc.dram_tensor("x_out", [D, S], F32,
                                    kind=x_out_kind).ap()
    for nm in ("wq", "wk", "wv", "wo"):
        d[nm] = nc.dram_tensor("c_" + nm, [D, D], F32,
                               kind="ExternalInput").ap()
    d["w1"] = nc.dram_tensor("c_w1", [D, 4 * D], F32,
                             kind="ExternalInput").ap()
    d["w2"] = nc.dram_tensor("c_w2", [4 * D, D], F32,
                             kind="ExternalInput").ap()
    for nm in ("ln1s_c", "ln1bc_c", "ln2s_c", "ln2bc_c", "b2_c"):
        d[nm] = nc.dram_tensor("c_" + nm, [128, NCF], F32,
                               kind="ExternalInput").ap()
    d["b1_c"] = nc.dram_tensor("c_b1_c", [128, 32], F32,
                               kind="ExternalInput").ap()
    d["triumask"] = nc.dram_tensor("c_triu", [128, 128], F32,
                                   kind="ExternalInput").ap()
    d["w1_bf"] = nc.dram_tensor("c_w1_bf", [D, 4 * D], BF16,
                                kind="Internal").ap()
    d["w2_bf"] = nc.dram_tensor("c_w2_bf", [4 * D, D], BF16,
                                kind="Internal").ap()
    for nm in ("wq", "wk", "wv"):
        d[nm + "_bf"] = nc.dram_tensor("c_" + nm + "_bf", [D, D], BF16,
                                       kind="Internal").ap()
    return d


def prep_c_inputs(inputs, S):
    g = lambda k: np.asarray(inputs["L1_attn_" + k])
    m = {}
    m["c_wq"] = np.ascontiguousarray(g("wq").reshape(D, D), dtype=np.float32)
    m["c_wk"] = np.ascontiguousarray(g("wk").reshape(D, D), dtype=np.float32)
    m["c_wv"] = np.ascontiguousarray(g("wv").reshape(D, D), dtype=np.float32)
    m["c_wo"] = np.ascontiguousarray(g("wo").reshape(D, D), dtype=np.float32)
    m["c_w1"] = np.ascontiguousarray(g("w1"), dtype=np.float32)
    m["c_w2"] = np.ascontiguousarray(g("w2"), dtype=np.float32)
    m["c_ln1s_c"] = _col_layout(g("ln1_scale"))
    m["c_ln1bc_c"] = _col_layout(g("ln1_bias"))
    m["c_ln2s_c"] = _col_layout(g("ln2_scale"))
    m["c_ln2bc_c"] = _col_layout(g("ln2_bias"))
    m["c_b2_c"] = _col_layout(g("b2"))
    m["c_b1_c"] = np.ascontiguousarray(
        g("b1").reshape(32, 128).T, dtype=np.float32)
    m["c_triu"] = np.triu(np.ones((128, 128))).astype(np.float32)
    return m


def build_block_c(S):
    nc = bass.Bass("TRN2", target_bir_lowering=False)
    with ExitStack() as ctx:
        dram = declare_dram_c(nc, S)
        B = make_common_bufs(nc, ctx, S)
        add_c_bufs(B)
        psf, psb = make_psum(nc, ctx)
        p = Prog(nc, ctx)
        load_consts(p, B, dram)
        emit_block_c(p, B, psf, psb, dram, S)
        p.finalize()
    return nc


def add_d_bufs(B):
    B.add("g32", [128, 32], F32)
    B.add("sfc", [128, 32], F32)
    B.add("sm32", [128, 32], F32)
    B.add("cur32", [128, 32], F32)
    B.add("k32", [128, 32], F32)
    B.add("cbc", [128, 512], BF16)
    B.add("gw_s32", [128, 32 * NCF], F32)
    B.add("ebc", [128, 32], F32)
    B.t["dnw"] = B.t["at_t"]
    B.t["din_d"] = B.t["xa_tA"]
    B.t["hh_d"] = B.t["P0"]
    B.t["sg_d"] = B.t["P1"]
    B.t["uu_d"] = B.t["PT0"]


def emit_block_d(p, B, psf, psb, dram, S):
    NTG = S // 512
    psb0f = psb[0][:].bitcast(F32)
    psb1f = psb[1][:].bitcast(F32)
    p.begin_params()
    p.dma("sp", B["gw_s32"], dram["gw_T32_pc"])
    p.dma("sp", B["ebc"], dram["ebc"])
    p.end_params()

    for g in range(NTG):
        cols = slice(512 * g, 512 * (g + 1))
        load_group(p, B, dram["x_in"], cols)
        ln_stats_group(p, B, psf, "xg", eps=1e-5)
        bcast_rows(p, B, psf[2][:, :], B["r_row_hi"][0:1, :],
                   B["r_row_lo"][0:1, :])
        bcast_rows(p, B, psf[3][:, :], B["mr_row_hi"][0:1, :],
                   B["mr_row_lo"][0:1, :])
        for c in range(NCF):
            sl = slice(512 * c, 512 * (c + 1))
            rows = slice(128 * c, 128 * (c + 1))
            p.tt(B["t1"], B["xg"][:, sl], psf[2][:, :], ALU.mult)
            p.tt(B["hbuf"][:, sl], B["t1"], psf[3][:, :], ALU.subtract)
            p.copy(B["xab"], B["hbuf"][:, sl])
            p.dma("sp", dram["din_sp"][rows, cols], B["xab"])
        for tt_ in range(4):
            tcols = slice(128 * tt_, 128 * (tt_ + 1))
            for c in range(NCF):
                p.mm(psf[0][:, 0:32],
                     B["hbuf"][:, 512 * c + 128 * tt_:
                               512 * c + 128 * (tt_ + 1)],
                     B["gw_s32"][:, 32 * c:32 * (c + 1)],
                     start=(c == 0), stop=(c == NCF - 1))
            p.act(B["g32"], psf[0][:, 0:32], AF.Sigmoid)
            p.tt(B["sfc"], B["g32"], B["ebc"], ALU.add)
            p.red(B["gs8"], B["sfc"][:].rearrange("p (g f) -> p g f", f=4),
                  ALU.max)
            p.copy(B["cur8"], B["gs8"])
            for i in range(4):
                p.red(B["m1c"], B["cur8"], ALU.max)
                if i < 3:
                    p.ts(B["gk"][:, 0:8], B["cur8"], B["m1c"][:, 0:1],
                         op0=ALU.is_ge)
                    p.stt(B["cur8"], B["gk"][:, 0:8], -1e9, B["cur8"],
                          ALU.mult, ALU.add)
            p.ts(B["gk"][:, 0:8], B["gs8"], B["m1c"][:, 0:1], op0=ALU.is_ge)
            for i in range(4):
                p.copy(B["sm32"][:, i:32:4], B["gk"][:, 0:8])
            p.tt(B["sm32"], B["sfc"], B["sm32"], ALU.mult)
            p.copy(B["cur32"], B["sm32"])
            for i in range(6):
                p.red(B["m2c"], B["cur32"], ALU.max)
                if i < 5:
                    p.ts(B["k32"], B["cur32"], B["m2c"][:, 0:1],
                         op0=ALU.is_ge)
                    p.stt(B["cur32"], B["k32"], -1e9, B["cur32"],
                          ALU.mult, ALU.add)
            p.ts(B["k32"], B["sm32"], B["m2c"][:, 0:1], op0=ALU.is_ge)
            p.tt(B["k32"], B["g32"], B["k32"], ALU.mult)
            p.red(B["gs1"], B["k32"], ALU.add)
            p.ts(B["gs1"], B["gs1"], 1e-20, op0=ALU.add)
            p.recip(B["gr1"], B["gs1"])
            p.ts(B["k32"], B["k32"], B["gr1"][:, 0:1], 2.5,
                 op0=ALU.mult, op1=ALU.mult)
            p.transp(psf[1][0:32, 0:128], B["k32"], B["ident_f"])
            p.copy(B["cTx"][0:32, tcols], psf[1][0:32, 0:128])
        p.dma("sp", dram["c_dram32"][:, cols], B["cTx"][0:32, :])
    p.fence()

    for j in range(S // 512):
        jc = slice(512 * j, 512 * (j + 1))
        load_2d(p, B["din_d"][:, 0:NCF * 512], dram["din_sp"][:, jc],
                "act")
        p.memset(B["hbuf"][:, 512 * 6:512 * 8], 0.0, eng="dve")
        for e in range(34):
            shared = e >= 32
            gu_buf = (B["wob"][:, 0:NCF * D] if e % 2 == 0 else
                      B["xg"][:].bitcast(BF16)[:, 0:NCF * D])
            dn_buf = (B["dnw"][:, 0:4 * D] if e % 2 == 0 else
                      B["hbuf"][:, 0:2048].bitcast(BF16)[:, 0:4 * D])
            if not shared:
                p.dma("sp", B["crow"], dram["c_dram32"][e:e + 1, jc])
                p.copy(B["crh"], B["crow"])
                p.tt(B["crl"], B["crow"], B["crh"], ALU.subtract)
                bcast_rows(p, B, psb0f[:, 0:512], B["crh"][0:1, :],
                           B["crl"][0:1, :])
                p.copy(B["cbc"], psb0f[:, 0:512])
            if CV is not None:
                src_gu = (dram["shgu_bf"][e - 32] if shared
                          else dram["gu_bf"][e])
                load_2d(p, gu_buf, src_gu, "sp" if e % 2 == 0 else "act")
                src_dn = (dram["shdn_bf"][e - 32] if shared
                          else dram["dn_bf"][e])
                load_2d(p, dn_buf, src_dn, "act" if e % 2 == 0 else "sp")
            else:
                src_gu = (dram["shgu_T"][e - 32] if shared
                          else dram["gu_T"][e])
                load_2d(p, gu_buf, src_gu, "pool")
                src_dn = (dram["shdn_T"][e - 32] if shared
                          else dram["dn_T"][e])
                load_2d(p, dn_buf, src_dn, "pool")
            for ic in range(4):
                gb = psb0f
                ub = psb1f
                for c in range(NCF):
                    p.mm(gb[:, 0:512],
                         gu_buf[:, 1024 * c + 128 * ic:
                                1024 * c + 128 * (ic + 1)],
                         B["din_d"][:, 512 * c:512 * (c + 1)],
                         start=(c == 0), stop=(c == NCF - 1))
                    p.mm(ub[:, 0:512],
                         gu_buf[:, 1024 * c + 512 + 128 * ic:
                                1024 * c + 512 + 128 * (ic + 1)],
                         B["din_d"][:, 512 * c:512 * (c + 1)],
                         start=(c == 0), stop=(c == NCF - 1))
                p.act(B["sg_d"][:, 512 * ic:512 * (ic + 1)], gb[:, 0:512],
                      AF.Silu)
                p.copy(B["uu_d"][:, 512 * ic:512 * (ic + 1)], ub[:, 0:512])
                p.tt(B["hh_d"][:, 512 * ic:512 * (ic + 1)],
                     B["sg_d"][:, 512 * ic:512 * (ic + 1)],
                     B["uu_d"][:, 512 * ic:512 * (ic + 1)], ALU.mult)
                if not shared:
                    p.tt(B["hh_d"][:, 512 * ic:512 * (ic + 1)],
                         B["hh_d"][:, 512 * ic:512 * (ic + 1)],
                         B["cbc"], ALU.mult)
            for oc in (6, 7, 0, 1, 2, 3, 4, 5):
                if oc < 6:
                    bank = psf[oc]
                    for ic in range(4):
                        p.mm(bank,
                             dn_buf[:, 1024 * ic + 128 * oc:
                                    1024 * ic + 128 * (oc + 1)],
                             B["hh_d"][:, 512 * ic:512 * (ic + 1)],
                             start=(e == 0 and ic == 0),
                             stop=(e == 33 and ic == 3))
                else:
                    bank = psb0f if oc == 6 else psb1f
                    for ic in range(4):
                        p.mm(bank[:, 0:512],
                             dn_buf[:, 1024 * ic + 128 * oc:
                                    1024 * ic + 128 * (oc + 1)],
                             B["hh_d"][:, 512 * ic:512 * (ic + 1)],
                             start=(ic == 0), stop=(ic == 3))
                    p.tt(B["hbuf"][:, 512 * oc:512 * (oc + 1)],
                         B["hbuf"][:, 512 * oc:512 * (oc + 1)],
                         bank[:, 0:512], ALU.add)
        load_group(p, B, dram["x_in"], jc)
        for oc in range(NCF):
            rows = slice(128 * oc, 128 * (oc + 1))
            if oc < 6:
                p.tt(B["x1o"], psf[oc][:, :], B["xg"][:, 512 * oc:
                                                      512 * (oc + 1)],
                     ALU.add)
            else:
                p.tt(B["x1o"], B["hbuf"][:, 512 * oc:512 * (oc + 1)],
                     B["xg"][:, 512 * oc:512 * (oc + 1)], ALU.add)
            p.dma("sp", dram["x_out"][rows, jc], B["x1o"])
    p.fence()


def declare_dram_d(nc, S, x_in_kind="ExternalInput",
                   x_out_kind="ExternalOutput", consts=None):
    d = dict(consts) if consts else declare_consts(nc)
    if "x_in" not in d:
        d["x_in"] = nc.dram_tensor("x_in", [D, S], F32, kind=x_in_kind).ap()
        d["x_out"] = nc.dram_tensor("x_out", [D, S], F32,
                                    kind=x_out_kind).ap()
    d["gw_T32_pc"] = nc.dram_tensor("d_gw_pc", [128, 32 * NCF], F32,
                                    kind="ExternalInput").ap()
    d["ebc"] = nc.dram_tensor("d_ebc", [128, 32], F32,
                              kind="ExternalInput").ap()
    d["gu_T"] = nc.dram_tensor("d_gu_T", [32, D, D], F32,
                               kind="ExternalInput").ap()
    d["dn_T"] = nc.dram_tensor("d_dn_T", [32, 512, D], F32,
                               kind="ExternalInput").ap()
    d["shgu_T"] = nc.dram_tensor("d_shgu_T", [2, D, D], F32,
                                 kind="ExternalInput").ap()
    d["shdn_T"] = nc.dram_tensor("d_shdn_T", [2, 512, D], F32,
                                 kind="ExternalInput").ap()
    d["gu_bf"] = nc.dram_tensor("d_gu_bf", [32, D, D], BF16,
                                kind="Internal").ap()
    d["dn_bf"] = nc.dram_tensor("d_dn_bf", [32, 512, D], BF16,
                                kind="Internal").ap()
    d["shgu_bf"] = nc.dram_tensor("d_shgu_bf", [2, D, D], BF16,
                                  kind="Internal").ap()
    d["shdn_bf"] = nc.dram_tensor("d_shdn_bf", [2, 512, D], BF16,
                                  kind="Internal").ap()
    d["din_sp"] = nc.dram_tensor("d_din_sp", [D, S], BF16,
                                 kind="Internal").ap()
    d["c_dram32"] = nc.dram_tensor("d_c32", [32, S], F32,
                                   kind="Internal").ap()
    return d


def prep_d_inputs(inputs, S):
    g = lambda k: np.asarray(inputs["L1_ffn_" + k])
    m = {}
    gw = g("gate_weight")
    gwT = gw.T
    m["d_gw_pc"] = np.ascontiguousarray(
        gwT.reshape(NCF, 128, 32).transpose(1, 0, 2).reshape(128, NCF * 32),
        dtype=np.float32)
    m["d_ebc"] = np.ascontiguousarray(
        np.broadcast_to(g("e_score_correction_bias")[None, :], (128, 32)),
        dtype=np.float32)
    m["d_gu_T"] = np.ascontiguousarray(
        g("experts_gate_up").transpose(0, 2, 1), dtype=np.float32)
    m["d_dn_T"] = np.ascontiguousarray(
        g("experts_down").transpose(0, 2, 1), dtype=np.float32)
    sg = g("shared_gate_w")
    su = g("shared_up_w")
    sd_ = g("shared_down_w")
    sgT = sg.T
    suT = su.T
    sdT = sd_.T
    shgu = np.zeros((2, D, D), dtype=np.float32)
    shdn = np.zeros((2, 512, D), dtype=np.float32)
    for s_ in range(2):
        shgu[s_, :, 0:512] = sgT[:, 512 * s_:512 * (s_ + 1)]
        shgu[s_, :, 512:1024] = suT[:, 512 * s_:512 * (s_ + 1)]
        shdn[s_] = sdT[512 * s_:512 * (s_ + 1), :]
    m["d_shgu_T"] = shgu
    m["d_shdn_T"] = shdn
    return m


def build_block_d(S):
    nc = bass.Bass("TRN2", target_bir_lowering=False)
    with ExitStack() as ctx:
        dram = declare_dram_d(nc, S)
        B = make_common_bufs(nc, ctx, S)
        B.add("gs8", [128, 8], F32)
        B.add("cur8", [128, 8], F32)
        for nm, sh in (("g8e", [128, 8]), ("gs1", [128, 1]),
                       ("gr1", [128, 1]), ("gng", [128, 1])):
            B.add(nm, sh, F32)
        add_d_bufs(B)
        psf, psb = make_psum(nc, ctx)
        p = Prog(nc, ctx)
        load_consts(p, B, dram)
        emit_block_d(p, B, psf, psb, dram, S)
        p.finalize()
    return nc


def kernel(**inputs):
    return kernel_fused(**inputs)


def add_bcd_small_bufs(B):
    B.add("gw_s", [128, 8 * NCF], F32)
    B.add("gs8", [128, 8], F32)
    B.add("cur8", [128, 8], F32)
    for nm in ("g8e", "gs1", "gr1", "gng"):
        B.add(nm, [128, 8] if nm == "g8e" else [128, 1], F32)


def build_fused(S):
    nc = bass.Bass("TRN2", target_bir_lowering=False)
    with ExitStack() as ctx:
        consts = declare_consts(nc)
        xin = nc.dram_tensor("x_in", [D, S], F32, kind="ExternalInput").ap()
        xout = nc.dram_tensor("x_out", [D, S], F32,
                              kind="ExternalOutput").ap()
        xs1 = nc.dram_tensor("xs1", [D, S], F32, kind="Internal").ap()
        xs2 = nc.dram_tensor("xs2", [D, S], F32, kind="Internal").ap()
        xs3 = nc.dram_tensor("xs3", [D, S], F32, kind="Internal").ap()
        dA = declare_dram_a(nc, S, consts=dict(consts, x_in=xin, x_out=xs1))
        dB = declare_dram_b(nc, S, consts=dict(consts, x_in=xs1, x_out=xs2))
        dC = declare_dram_c(nc, S, consts=dict(consts, x_in=xs2, x_out=xs3))
        dD = declare_dram_d(nc, S, consts=dict(consts, x_in=xs3,
                                               x_out=xout))
        B = make_common_bufs(nc, ctx, S)
        add_bcd_small_bufs(B)
        add_c_bufs(B)
        add_d_bufs(B)
        B.t["cvstg"] = B.t["P0"]
        psf, psb = make_psum(nc, ctx)
        p = Prog(nc, ctx)
        load_consts(p, B, consts)
        global CV
        cv = Converter(p, B)
        for e in range(8):
            cv.add(dB["exp_wT"][e], dB["exp_bf"][e])
        cv.add(dC["w1"], dC["w1_bf"])
        cv.add(dC["w2"], dC["w2_bf"])
        for nm in ("wq", "wk", "wv"):
            cv.add(dC[nm], dC[nm + "_bf"])
        for e in range(32):
            cv.add(dD["gu_T"][e], dD["gu_bf"][e])
            cv.add(dD["dn_T"][e], dD["dn_bf"][e])
        for e in range(2):
            cv.add(dD["shgu_T"][e], dD["shgu_bf"][e])
            cv.add(dD["shdn_T"][e], dD["shdn_bf"][e])
        CV = cv
        try:
            emit_block_a(p, B, psf, psb, dA, S)
            emit_block_b(p, B, psf, psb, dB, S)
            cv.done()
            emit_block_c(p, B, psf, psb, dC, S)
            emit_block_d(p, B, psf, psb, dD, S)
            p.finalize()
        finally:
            CV = None
    return nc


def kernel_fused(**inputs):
    S = 4096
    x = np.asarray(inputs["L0_attn_x"], dtype=np.float32)
    m = const_arrays()
    for prep in (prep_a_inputs, prep_b_inputs, prep_c_inputs,
                 prep_d_inputs):
        m.update(prep(inputs, S))
    nc = build_fused(S)
    in_maps = [dict(m, x_in=np.ascontiguousarray(x[b].T))
               for b in range(N_CORES)]
    res = run_bass_kernel_spmd(nc, in_maps, core_ids=list(range(N_CORES)))
    return np.ascontiguousarray(
        np.stack([np.asarray(r["x_out"]).T
                  for r in res.results]).astype(np.float32))


# version 14 (final)
# speedup vs baseline: 1.1442x; 1.1442x over previous
import numpy as np
import ml_dtypes
from contextlib import ExitStack

import concourse.bass as bass
import concourse.mybir as mybir
from concourse.bass_utils import run_bass_kernel_spmd

F32 = mybir.dt.float32
BF16 = mybir.dt.bfloat16
AF = mybir.ActivationFunctionType
ALU = mybir.AluOpType
AX = mybir.AxisListType
BF = ml_dtypes.bfloat16

N_CORES = 8
D = 1024
NH = 16
DH = 64
NPAIR = NH // 2
NCF = D // 128


def _apify(x):
    return x if isinstance(x, bass.AP) else x[:]


def _name(ap):
    return ap.tensor.name


_BAND = 1024


def _slots(ap):
    t = ap.tensor
    nm = t.name
    if "DRAM" in str(ap.space):
        return ((nm, -1),)
    apl = ap.ap
    esz = mybir.dt.size(ap.dtype)
    rowlen = apl[0][0]
    free_lo = ap.offset - ap.base_partition() * rowlen
    span = 1
    for step, count in apl[1:]:
        span += abs(step) * (count - 1)
    b0 = (free_lo * esz) // _BAND
    b1 = ((free_lo + span) * esz - 1) // _BAND
    return tuple((nm, b) for b in range(b0, b1 + 1))


class Tok:
    __slots__ = ("sem", "val", "used", "is_dma")

    def __init__(self, sem, val=None, is_dma=False):
        self.sem = sem
        self.val = val
        self.used = False
        self.is_dma = is_dma


class _Item:
    __slots__ = ("eng", "fn", "tok", "waits", "inc_amt", "do_inc")

    def __init__(self, eng, fn, tok, waits, inc_amt):
        self.eng = eng
        self.fn = fn
        self.tok = tok
        self.waits = waits
        self.inc_amt = inc_amt
        self.do_inc = False


class Prog:
    ENGINES = ("pe", "act", "dve", "pool", "sp")

    def __init__(self, nc, ctx):
        self.nc = nc
        self.ctx = ctx
        self.items = []
        self.w = {}
        self.r = {}
        self.sem_handles = {}
        self.stream_cnt = {}
        self.last_item = {}
        self.last_dma_tok = {}
        self.param_mode = False
        self._param_recs = []

    def _sem(self, key):
        if key not in self.sem_handles:
            self.sem_handles[key] = self.ctx.enter_context(
                self.nc.semaphore(f"s_{key}"))
        return self.sem_handles[key]

    def _emit(self, eng, fn, rd, wr, extra_waits=(), sem=None, dma=False):
        waits = list(extra_waits)
        rd = [s for a in rd if a is not None for s in _slots(a)]
        wr = [s for a in wr if a is not None for s in _slots(a)]
        for s in rd:
            t = self.w.get(s)
            if t is not None:
                waits.append(t)
        if dma and sem is None and self.param_mode:
            sem = "prmp" if eng == "pool" else "prm"
        semkey = sem if sem is not None else eng
        for s in wr:
            t = self.w.get(s)
            if t is not None and not (dma and t.is_dma and t.sem == semkey):
                waits.append(t)
            waits.extend(self.r.get(s, ()))
        if dma:
            cum = self.stream_cnt.get(semkey, 0) + 16
            self.stream_cnt[semkey] = cum
            tok = Tok(semkey, cum, is_dma=True)
            inc_amt = 16
            self.last_dma_tok[semkey] = tok
        else:
            tok = Tok(semkey)
            inc_amt = 1
        it = _Item(eng, fn, tok, waits, inc_amt)
        self.items.append(it)
        self.last_item[eng] = it
        for s in rd:
            self.r.setdefault(s, []).append(tok)
        for s in wr:
            self.w[s] = tok
            self.r[s] = []
        if dma and self.param_mode:
            self._param_recs.append((tok, wr))
        return tok

    def begin_params(self):
        self.param_mode = True
        self._param_recs = []

    def end_params(self):
        self.param_mode = False
        finals = {}
        for tok, wr in self._param_recs:
            finals[tok.sem] = tok
        for tok, wr in self._param_recs:
            for s in wr:
                if self.w.get(s) is tok:
                    self.w[s] = finals[tok.sem]
        self._param_recs = []

    def mm(self, out, lhsT, rhs, start=True, stop=True, tile_position=None):
        out, lhsT, rhs = _apify(out), _apify(lhsT), _apify(rhs)
        return self._emit(
            "pe", lambda t: t.matmul(out, lhsT, rhs, start=start, stop=stop,
                                     tile_position=tile_position),
            [lhsT, rhs], [out])

    def transp(self, out, in_, ident):
        out, in_, ident = _apify(out), _apify(in_), _apify(ident)
        return self._emit("pe", lambda t: t.transpose(out, in_, ident),
                          [in_, ident], [out])

    def act(self, out, in_, func, bias=0.0, scale=1.0, accum=None):
        out, in_ = _apify(out), _apify(in_)
        b = _apify(bias) if isinstance(bias, (bass.AP,)) or hasattr(bias, "tensor") or not isinstance(bias, (int, float)) else bias
        s = _apify(scale) if not isinstance(scale, (int, float)) else scale
        acc = _apify(accum) if accum is not None else None
        rd = [in_]
        if not isinstance(b, (int, float)):
            rd.append(b)
        if not isinstance(s, (int, float)):
            rd.append(s)
        wr = [out] + ([acc] if acc is not None else [])
        if acc is not None:
            return self._emit("act", lambda e: e.activation(
                out, in_, func, bias=b, scale=s, accum_out=acc), rd, wr)
        return self._emit("act", lambda e: e.activation(
            out, in_, func, bias=b, scale=s), rd, wr)

    def tt(self, out, in0, in1, op, eng="dve"):
        out, in0, in1 = _apify(out), _apify(in0), _apify(in1)
        return self._emit(eng, lambda e: e.tensor_tensor(
            out=out, in0=in0, in1=in1, op=op), [in0, in1], [out])

    def ts(self, out, in0, s1, s2=None, op0=ALU.mult, op1=ALU.bypass,
           eng="dve"):
        out, in0 = _apify(out), _apify(in0)
        s1a = _apify(s1) if not isinstance(s1, (int, float)) else s1
        s2a = _apify(s2) if s2 is not None and not isinstance(s2, (int, float)) else s2
        rd = [in0]
        if not isinstance(s1a, (int, float)):
            rd.append(s1a)
        if s2a is not None and not isinstance(s2a, (int, float)):
            rd.append(s2a)
        if s2 is None:
            return self._emit(eng, lambda e: e.tensor_single_scalar(
                out=out, in_=in0, scalar=s1a, op=op0), rd, [out])
        return self._emit(eng, lambda e: e.tensor_scalar(
            out=out, in0=in0, scalar1=s1a, scalar2=s2a, op0=op0, op1=op1),
            rd, [out])

    def stt(self, out, in0, s, in1, op0, op1, eng="dve"):
        out, in0, in1 = _apify(out), _apify(in0), _apify(in1)
        sa = _apify(s) if not isinstance(s, (int, float)) else s
        rd = [in0, in1] + ([sa] if not isinstance(sa, (int, float)) else [])
        return self._emit(eng, lambda e: e.scalar_tensor_tensor(
            out=out, in0=in0, scalar=sa, in1=in1, op0=op0, op1=op1),
            rd, [out])

    def copy(self, out, in_, eng="dve"):
        out, in_ = _apify(out), _apify(in_)
        return self._emit(eng, lambda e: e.tensor_copy(out=out, in_=in_),
                          [in_], [out])

    def red(self, out, in_, op, eng="dve", negate=None):
        out, in_ = _apify(out), _apify(in_)
        return self._emit(eng, lambda e: e.tensor_reduce(
            out=out, in_=in_, axis=AX.X, op=op, negate=negate), [in_], [out])

    def recip(self, out, in_):
        out, in_ = _apify(out), _apify(in_)
        return self._emit("dve", lambda e: e.reciprocal(out=out, in_=in_),
                          [in_], [out])

    def shuffle(self, out, in_, mask):
        out, in_ = _apify(out), _apify(in_)
        return self._emit("dve", lambda e: e.stream_shuffle(
            out=out, in_=in_, mask=mask), [in_], [out])

    def memset(self, ap, val, eng="pool"):
        ap = _apify(ap)
        return self._emit(eng, lambda e: e.memset(ap, val), [], [ap])

    def dma(self, queue, out, in_, sem=None):
        out, in_ = _apify(out), _apify(in_)
        if sem is None:
            sb = out if str(out.tensor.space).find("DRAM") < 0 else in_
            b0 = _slots(sb)[0][1]
            sem = ("dp_" if queue == "pool" else "d_") + _name(sb) + "_" + str(b0)
        return self._emit(queue, lambda e: e.dma_start(out=out, in_=in_),
                          [in_], [out], sem=sem, dma=True)

    FENCES_ON = False

    def fence(self):
        if not self.FENCES_ON:
            return
        toks = []
        for e in self.ENGINES:
            it = self.last_item.get(e)
            if it is not None:
                toks.append(it.tok)
        toks.extend(self.last_dma_tok.values())
        for e in self.ENGINES:
            self._emit(e, lambda eng: eng.nop(), [], [], extra_waits=toks)

    def finalize(self, final_toks=()):
        self._emit("sp", lambda e: e.nop(), [], [],
                   extra_waits=list(self.last_dma_tok.values()))
        for it in self.items:
            for t in it.waits:
                t.used = True
        CAP = 48000
        counts, seg = {}, {}
        for it in self.items:
            if not (it.tok.is_dma or it.tok.used):
                continue
            it.do_inc = True
            base = it.tok.sem
            amt = it.inc_amt
            if counts.get(base, 0) + amt > CAP:
                seg[base] = seg.get(base, 0) + 1
                counts[base] = 0
            c = counts.get(base, 0) + amt
            counts[base] = c
            it.tok.val = c
            if seg.get(base, 0):
                it.tok.sem = f"{base}@{seg[base]}"
        per_eng = {e: [] for e in self.ENGINES}
        for it in self.items:
            per_eng[it.eng].append(it)
        for it in self.items:
            self._sem(it.tok.sem)
            for t in it.waits:
                self._sem(t.sem)
        block = self.ctx.enter_context(self.nc.Block())
        guaranteed = {e: {} for e in self.ENGINES}

        def body(eng_name):
            def run(eng_obj):
                g = guaranteed[eng_name]
                for it in per_eng[eng_name]:
                    need = {}
                    for t in it.waits:
                        if t.val is None:
                            continue
                        if eng_name == "pe" and (t.sem == "pe" or
                                                 t.sem.startswith("pe@")):
                            continue
                        if need.get(t.sem, -1) < t.val:
                            need[t.sem] = t.val
                    for s, v in sorted(need.items()):
                        if g.get(s, -1) >= v:
                            continue
                        eng_obj.wait_ge(self._sem(s), v)
                        g[s] = v
                    inst = it.fn(eng_obj)
                    if it.do_inc and inst is not None:
                        inst.then_inc(self._sem(it.tok.sem), it.inc_amt)
            return run

        block.tensor(body("pe"))
        block.scalar(body("act"))
        block.vector(body("dve"))
        block.gpsimd(body("pool"))
        block.sync(body("sp"))


class BufSet:
    def __init__(self, nc, ctx):
        self.nc = nc
        self.ctx = ctx
        self.t = {}

    def add(self, name, shape, dtype):
        if name not in self.t:
            self.t[name] = self.ctx.enter_context(
                self.nc.sbuf_tensor("sb_" + name, shape, dtype))
        return self.t[name]

    def __getitem__(self, name):
        return self.t[name]


def make_common_bufs(nc, ctx, S):
    B = BufSet(nc, ctx)
    B.add("ident_b", [128, 128], BF16)
    B.add("ident_f", [128, 128], F32)
    B.add("ones1", [1, 128], BF16)
    B.add("ones1f", [1, 128], F32)
    B.add("onescol", [128, 1], BF16)
    B.add("ones2blk", [128, 2], BF16)
    B.add("xg", [128, NCF * 512], F32)
    B.add("xcb", [128, 512], BF16)
    B.add("sq2", [128, 512], BF16)
    B.add("hbuf", [128, NCF * 512], F32)
    B.add("wob", [128, NCF * D], BF16)
    B.add("xa_tA", [128, NCF * 512], BF16)
    B.add("xa_tB", [128, NCF * 512], BF16)
    B.add("at_t", [128, NCF * 512], BF16)
    SW = max(S, NCF * 512)
    B.add("qf", [128, SW], BF16)
    B.add("kf", [128, SW], BF16)
    B.add("vtm", [128, max(SW, 4352)], BF16)
    B.add("P0", [128, SW], BF16)
    B.add("P1", [128, SW], BF16)
    B.add("PT0", [128, SW], BF16)
    B.add("PT1", [128, SW], BF16)
    B.add("wqp", [128, D], BF16)
    B.add("wkp", [128, D], BF16)
    B.add("wvp", [128, D], BF16)
    B.add("t1", [128, 512], F32)
    B.add("t2", [128, 512], F32)
    B.add("t3", [128, 512], F32)
    B.t["pe_t"] = B.t["t3"]
    B.add("xab", [128, 512], BF16)
    B.add("x1o", [128, 512], F32)
    B.add("attn_t", [128, 128], F32)
    B.add("attn_o", [128, 128], BF16)
    for nm in ("m_row", "q_row", "t_row", "r_row", "mr_row"):
        B.add(nm, [1, 512], F32)
    for nm in ("r_row_hi", "r_row_lo", "mr_row_hi", "mr_row_lo"):
        B.add(nm, [1, 512], BF16)
    B.add("bv_row", [1, D], F32)
    B.add("bvr_hi", [1, D], BF16)
    B.add("bvr_lo", [1, D], BF16)
    B.add("sums", [128, 16], F32)
    B.add("scol", [128, 2], F32)
    B.add("rcol", [128, 2], F32)
    B.add("rcol_hi", [128, 2], BF16)
    B.add("rcol_lo", [128, 2], BF16)
    for i in range(4):
        B.add("rrow%d" % i, [1, 128], BF16)
    B.add("bcol", [128, 2], F32)
    B.add("triumask", [128, 128], F32)
    B.add("dnr", [1, 128], F32)
    B.add("dnrh", [1, 128], BF16)
    B.add("dnrl", [1, 128], BF16)
    B.t["numt"] = B.t["attn_t"]
    B.add("nrmt", [2, 16], F32)
    B.add("nrm", [2, 2], F32)
    B.add("nrm_s", [2, 2], F32)
    B.add("bneg", [2, 1], F32)
    B.add("bnrow", [1, 2], BF16)
    B.add("bneg_b", [2, 1], BF16)
    for nm in ("bq_c", "bk_c", "bv_c", "bo_c", "gamma_c", "beta_c"):
        B.add("pc_" + nm, [128, NCF], F32)
    B.add("trilmask", [128, 128], BF16)
    B.add("ropeA_t", [128, 512], F32)
    B.add("ropeB_t", [128, 512], F32)
    B.add("swp", [128, 512], BF16)
    B.add("qro", [128, 512], BF16)
    B.add("g8", [128, 8], F32)
    B.add("g8b", [128, 8], F32)
    B.add("m1c", [128, 1], F32)
    B.add("m2c", [128, 1], F32)
    B.add("gk", [128, 8], F32)
    B.add("cTx", [32, 512], F32)
    B.add("cT8b", [8, 512], BF16)
    B.add("crow", [1, 512], F32)
    B.add("crh", [1, 512], BF16)
    B.add("crl", [1, 512], BF16)
    B.add("gbh", [1, 8], BF16)
    B.add("gbl", [1, 8], BF16)
    B.add("gbrow", [1, 8], F32)
    B.add("expb8", [8, D], BF16)
    for nm in ("ln1w_c", "ln1b_c", "ln2w_c", "ln2b_c"):
        B.add("pc_" + nm, [128, NCF], F32)
    return B


def make_psum(nc, ctx):
    psf = [ctx.enter_context(nc.psum_tensor(f"psf{i}", [128, 512], F32))
           for i in range(6)]
    psb = [ctx.enter_context(nc.psum_tensor(f"psb{i}", [128, 1024], BF16))
           for i in range(2)]
    return psf, psb


CONST_NAMES = ("ident_b", "ident_f", "ones1", "ones1f", "onescol",
               "ones2blk")


def load_consts(p, B, dram):
    p.begin_params()
    for nm in CONST_NAMES:
        p.dma("pool", B[nm], dram[nm])
    p.end_params()


class Converter:

    UCOLS = 2048

    def __init__(self, p, B):
        self.p = p
        self.B = B
        self.units = []
        self.k = 0

    def add(self, src2d, dst2d):
        R, C = src2d.tensor.shape[-2], src2d.tensor.shape[-1]
        if C > self.UCOLS:
            for r in range(0, R, 128):
                for c0 in range(0, C, self.UCOLS):
                    self.units.append(
                        (src2d[r:r + 128, c0:c0 + self.UCOLS],
                         dst2d[r:r + 128, c0:c0 + self.UCOLS]))
        else:
            rc = (128 * self.UCOLS) // C
            for r in range(0, R, rc):
                self.units.append((src2d[r:r + rc, :], dst2d[r:r + rc, :]))

    def pump(self, n):
        for _ in range(n):
            if self.k >= len(self.units):
                return
            src, dst = self.units[self.k]
            half = self.k % 2
            stg = self.B["cvstg"][:, self.UCOLS * half:
                                  self.UCOLS * (half + 1)]
            self.p.dma("pool", stg,
                       src.rearrange("(t p) n -> p t n", p=128))
            self.p.dma("sp", dst.rearrange("(t p) n -> p t n", p=128),
                       stg, sem="cvout")
            self.k += 1

    def done(self):
        self.pump(len(self.units) - self.k)


CV = None


def cv_pump(n):
    if CV is not None:
        CV.pump(n)


def load_group(p, B, dram_x, cols, buf="xg", queue="sp"):
    p.dma(queue, B[buf][:, 0:NCF * 512],
          dram_x[:, cols].rearrange("(t p) n -> p t n", p=128))


def load_2d(p, dst_ap, dram_2d, queue="pool"):
    p.dma(queue, dst_ap, dram_2d.rearrange("(t p) n -> p t n", p=128))


def row_stats(p, B, sum_ap, sumsq_ap, eps, n=512, extra_scale=None):
    m = B["m_row"][0:1, 0:n]
    q = B["q_row"][0:1, 0:n]
    t = B["t_row"][0:1, 0:n]
    r = B["r_row"][0:1, 0:n]
    mr = B["mr_row"][0:1, 0:n]
    p.ts(m, sum_ap, 1.0 / D)
    p.ts(q, sumsq_ap, 1.0 / D)
    p.tt(t, m, m, ALU.mult)
    p.stt(q, q, eps, t, ALU.add, ALU.subtract)
    p.act(t, q, AF.Sqrt)
    p.recip(r, t)
    if extra_scale is not None:
        extra_scale(p, B, n)
    p.tt(mr, m, r, ALU.mult)
    for nm in ("r_row", "mr_row"):
        src = B[nm][0:1, 0:n]
        p.copy(B[nm + "_hi"][0:1, 0:n], src)
        p.tt(B[nm + "_lo"][0:1, 0:n], src, B[nm + "_hi"][0:1, 0:n],
             ALU.subtract)


def bcast_rows(p, B, bank_ap, hi_ap, lo_ap):
    p.mm(bank_ap, B["ones1"], hi_ap, start=True, stop=False)
    p.mm(bank_ap, B["ones1"], lo_ap, start=False, stop=True)


def emit_block_a(p, B, psf, psb, dram, S):
    NQ = S // 128
    NKC = S // 512
    NTG = S // 512

    p.begin_params()
    for nm in ("bq_c", "bk_c", "bv_c", "bo_c", "gamma_c", "beta_c"):
        p.dma("sp", B["pc_" + nm], dram[nm])
    p.dma("sp", B["bv_row"], dram["bv_r"])
    p.end_params()
    p.copy(B["bvr_hi"], B["bv_row"])
    p.tt(B["bvr_lo"], B["bv_row"], B["bvr_hi"], ALU.subtract)

    for g in range(NTG):
        cols = slice(512 * g, 512 * (g + 1))
        load_group(p, B, dram["x_in"], cols)
        ln_stats_group(p, B, psf, "xg", eps=1e-5)
        bcast_rows(p, B, psf[2][:, :], B["r_row_hi"][0:1, :],
                   B["r_row_lo"][0:1, :])
        bcast_rows(p, B, psf[3][:, :], B["mr_row_hi"][0:1, :],
                   B["mr_row_lo"][0:1, :])
        for c in range(NCF):
            sl = slice(512 * c, 512 * (c + 1))
            rows = slice(128 * c, 128 * (c + 1))
            p.dma("sp", B["pe_t"], dram["peT"][rows, cols])
            p.tt(B["t1"], B["xg"][:, sl], psf[2][:, :], ALU.mult)
            p.tt(B["t2"], B["t1"], psf[3][:, :], ALU.subtract)
            p.tt(B["xab"], B["t2"], B["pe_t"], ALU.add)
            p.dma("sp", dram["xa_sp"][rows, cols], B["xab"])
    p.fence()

    init_vtm_ones(p, B, S)
    for pr in range(NPAIR):
        pcols = slice(128 * pr, 128 * (pr + 1))
        for wn in ("wq", "wk", "wv"):
            load_2d(p, B[wn + "p"][:, 0:D], dram[wn][:, pcols], "pool")
        for j in range(S // 512):
            jc = slice(512 * j, 512 * (j + 1))
            xa_t = B["xa_tA"] if j % 2 == 0 else B["xa_tB"]
            load_2d(p, xa_t[:, 0:NCF * 512], dram["xa_sp"][:, jc], "sp")
            for c in range(NCF):
                xac = xa_t[:, 512 * c:512 * (c + 1)]
                p.mm(psf[0], B["wqp"][:, 128 * c:128 * (c + 1)], xac,
                     start=(c == 0), stop=(c == NCF - 1))
                p.mm(psf[1], B["wkp"][:, 128 * c:128 * (c + 1)], xac,
                     start=(c == 0), stop=(c == NCF - 1))
            p.ts(B["qf"][:, jc], psf[0][:, :], B["pc_bq_c"][:, pr:pr + 1],
                 op0=ALU.add)
            p.ts(B["kf"][:, jc], psf[1][:, :], B["pc_bk_c"][:, pr:pr + 1],
                 op0=ALU.add)
            for tt_ in range(4):
                kt = 4 * j + tt_
                for c in range(NCF):
                    p.mm(psf[2][:, 0:128],
                         xa_t[:, 512 * c + 128 * tt_:512 * c + 128 * (tt_ + 1)],
                         B["wvp"][:, 128 * c:128 * (c + 1)],
                         start=(c == 0), stop=False)
                p.mm(psf[2][:, 0:128], B["ones1"],
                     B["bvr_hi"][0:1, pcols], start=False, stop=False)
                p.mm(psf[2][:, 0:128], B["ones1"],
                     B["bvr_lo"][0:1, pcols], start=False, stop=True)
                p.copy(B["vtm"][:, 132 * kt:132 * kt + 64], psf[2][:, 0:64])
                p.copy(B["vtm"][:, 132 * kt + 66:132 * kt + 130],
                       psf[2][:, 64:128])
        emit_score_bound(p, B, psf, S)
        emit_attn_qtile_T(p, B, psf, psb, dram, S, pr, causal=False)
        cv_pump(8)
    p.fence()

    load_2d(p, B["wob"][:, 0:NCF * D], dram["wo"][:, :], "pool")
    for j in range(S // 512):
        jc = slice(512 * j, 512 * (j + 1))
        atb = B["at_t"] if j % 2 == 0 else B["qf"]
        xab_ = B["xa_tA"] if j % 2 == 0 else B["kf"]
        load_2d(p, atb[:, 0:NCF * 512], dram["at_sp"][:, jc], "act")
        load_2d(p, xab_[:, 0:NCF * 512], dram["xa_sp"][:, jc], "sp")
        load_group(p, B, dram["x_in"], jc, queue="act")
        for oc in range(NCF):
            bank = psf[oc % 4]
            for c in range(NCF):
                p.mm(bank,
                     B["wob"][:, D * c + 128 * oc:D * c + 128 * (oc + 1)],
                     atb[:, 512 * c:512 * (c + 1)],
                     start=(c == 0), stop=(c == NCF - 1))
            p.ts(B["t1"], bank[:, :], B["pc_bo_c"][:, oc:oc + 1],
                 op0=ALU.add)
            p.tt(B["hbuf"][:, 512 * oc:512 * (oc + 1)], B["t1"],
                 xab_[:, 512 * oc:512 * (oc + 1)], ALU.add)
        ln_stats_group(p, B, psf, "hbuf", eps=1e-5)
        bcast_rows(p, B, psf[4][:, :], B["r_row_hi"][0:1, :],
                   B["r_row_lo"][0:1, :])
        bcast_rows(p, B, psf[5][:, :], B["mr_row_hi"][0:1, :],
                   B["mr_row_lo"][0:1, :])
        for oc in range(NCF):
            sl = slice(512 * oc, 512 * (oc + 1))
            rows = slice(128 * oc, 128 * (oc + 1))
            p.tt(B["t1"], B["hbuf"][:, sl], psf[4][:, :], ALU.mult)
            p.tt(B["t2"], B["t1"], psf[5][:, :], ALU.subtract)
            p.ts(B["t3"], B["t2"], B["pc_gamma_c"][:, oc:oc + 1],
                 B["pc_beta_c"][:, oc:oc + 1], op0=ALU.mult, op1=ALU.add)
            p.tt(B["x1o"], B["t3"], B["xg"][:, sl], ALU.add)
            p.dma("sp", dram["x_out"][rows, jc], B["x1o"])
        cv_pump(8)
    p.fence()


def init_vtm_ones(p, B, S):
    for kt in range(S // 128):
        p.memset(B["vtm"][:, 132 * kt + 64:132 * kt + 65], 1.0, eng="dve")
        p.memset(B["vtm"][:, 132 * kt + 130:132 * kt + 131], 1.0, eng="dve")


def emit_attn_qtile_T(p, B, psf, psb, dram, S, pr, causal):
    NKT = S // 128
    sc_banks = [(psf[0][:, :], psf[1][:, :]),
                (psb[0][:].bitcast(F32)[:, 0:512],
                 psb[1][:].bitcast(F32)[:, 0:512])]
    for qg in range(S // 512):
        qsl = slice(512 * qg, 512 * (qg + 1))
        nkt = (4 * qg + 4) if causal else NKT

        def scores(kt):
            ksl = slice(128 * kt, 128 * (kt + 1))
            b0, b1 = sc_banks[kt % 2]
            p.mm(b0, B["kf"][0:64, ksl], B["qf"][0:64, qsl])
            p.mm(b1, B["kf"][64:128, ksl], B["qf"][64:128, qsl])

        def expmask(kt):
            b0, b1 = sc_banks[kt % 2]
            loc = kt - 4 * qg
            po = 512 * (kt % 4)
            for hh, bb in ((0, b0), (1, b1)):
                PT = B["PT%d" % hh]
                if causal and loc >= 0:
                    z = 128 * loc
                    if z:
                        p.memset(PT[:, po:po + z], 0.0, eng="pool")
                    p.act(PT[:, po + z:po + 512], bb[:, z:512], AF.Exp,
                          scale=0.125, bias=B["bcol"][:, hh:hh + 1])
                    p.tt(PT[:, po + z:po + z + 128],
                         PT[:, po + z:po + z + 128], B["triumask"],
                         ALU.mult)
                else:
                    p.act(PT[:, po:po + 512], bb, AF.Exp, scale=0.125,
                          bias=B["bcol"][:, hh:hh + 1])

        def av(kt):
            po = 512 * (kt % 4)
            p.mm(psf[4][0:65, 0:512], B["vtm"][:, 132 * kt:132 * kt + 65],
                 B["PT0"][:, po:po + 512],
                 start=(kt == 0), stop=(kt == nkt - 1))
            p.mm(psf[5][0:65, 0:512],
                 B["vtm"][:, 132 * kt + 66:132 * kt + 131],
                 B["PT1"][:, po:po + 512],
                 start=(kt == 0), stop=(kt == nkt - 1))

        scores(0)
        for kt in range(nkt):
            expmask(kt)
            if kt + 1 < nkt:
                scores(kt + 1)
            av(kt)
        for hh, avb, obuf in ((0, psf[4], "qro"), (1, psf[5], "xab")):
            dn = B["m_row"][0:1, 0:512]
            dnh = B["r_row_hi"][0:1, 0:512]
            dnl = B["r_row_lo"][0:1, 0:512]
            p.recip(dn, avb[64:65, 0:512])
            p.copy(dnh, dn)
            p.tt(dnl, dn, dnh, ALU.subtract)
            p.mm(psf[2][0:64, 0:512], B["ones1"][0:1, 0:64], dnh,
                 start=True, stop=False)
            p.mm(psf[2][0:64, 0:512], B["ones1"][0:1, 0:64], dnl,
                 start=False, stop=True)
            p.copy(B["t1"][0:64, 0:512], avb[0:64, 0:512])
            p.tt(B[obuf][0:64, 0:512], B["t1"][0:64, 0:512],
                 psf[2][0:64, 0:512], ALU.mult)
            p.dma("sp", dram["at_sp"][128 * pr + 64 * hh:
                                      128 * pr + 64 * (hh + 1), qsl],
                  B[obuf][0:64, 0:512])


def emit_attn_qtile_T_old(p, B, psf, dram, S, pr, causal):
    NQ = S // 128
    pcols = slice(128 * pr, 128 * (pr + 1))
    for qi in range(NQ):
        qsl = slice(128 * qi, 128 * (qi + 1))
        nkt = (qi + 1) if causal else NQ
        for kg in range((nkt + 3) // 4):
            sb = kg % 2
            k0 = 4 * kg
            kts = range(k0, min(k0 + 4, nkt))
            for kt in kts:
                ksl = slice(128 * kt, 128 * (kt + 1))
                co = 128 * (kt - k0)
                p.mm(psf[2 * sb][:, co:co + 128],
                     B["kf"][0:64, ksl], B["qf"][0:64, qsl])
                p.mm(psf[2 * sb + 1][:, co:co + 128],
                     B["kf"][64:128, ksl], B["qf"][64:128, qsl])
            w = 128 * (len(list(kts)))
            for hh in range(2):
                p.act(B["PT%d" % hh][:, 128 * k0:128 * k0 + w],
                      psf[2 * sb + hh][:, 0:w], AF.Exp, scale=0.125,
                      bias=B["bcol"][:, hh:hh + 1])
            if causal and nkt - 1 in kts:
                dt_ = slice(128 * (nkt - 1), 128 * nkt)
                for hh in range(2):
                    p.tt(B["PT%d" % hh][:, dt_], B["PT%d" % hh][:, dt_],
                         B["triumask"], ALU.mult)
            for kt in kts:
                ksl = slice(128 * kt, 128 * (kt + 1))
                p.mm(psf[4][0:65, 0:128], B["vtm"][:, 132 * kt:132 * kt + 65],
                     B["PT0"][:, ksl], start=(kt == 0), stop=(kt == nkt - 1))
                p.mm(psf[5][0:65, 0:128],
                     B["vtm"][:, 132 * kt + 66:132 * kt + 131],
                     B["PT1"][:, ksl], start=(kt == 0), stop=(kt == nkt - 1))
        for hh, avb, bcb, obuf in ((0, psf[4], psf[0], "attn_o"),
                                   (1, psf[5], psf[1], "xab")):
            p.recip(B["dnr"], avb[64:65, 0:128])
            p.copy(B["dnrh"], B["dnr"])
            p.tt(B["dnrl"], B["dnr"], B["dnrh"], ALU.subtract)
            p.mm(bcb[0:64, 0:128], B["ones1"][0:1, 0:64], B["dnrh"],
                 start=True, stop=False)
            p.mm(bcb[0:64, 0:128], B["ones1"][0:1, 0:64], B["dnrl"],
                 start=False, stop=True)
            nt = B["numt"][0:64, :] if hh == 0 else B["attn_t"][0:64, :]
            p.copy(nt, avb[0:64, 0:128])
            p.tt(B[obuf][0:64, 0:128], nt, bcb[0:64, 0:128], ALU.mult)
            p.dma("sp", dram["at_sp"][128 * pr + 64 * hh:
                                      128 * pr + 64 * (hh + 1), qsl],
                  B[obuf][0:64, 0:128])


def emit_score_bound(p, B, psf, S):
    for tag, src in ((0, "qf"), (1, "kf")):
        for j in range(S // 512):
            sl = slice(512 * j, 512 * (j + 1))
            p.act(B["sq2"], B[src][:, sl], AF.Square)
            p.mm(psf[0][0:2, :], B["ones2blk"], B["sq2"])
            p.red(B["nrmt"][0:2, j:j + 1], psf[0][0:2, :], ALU.max)
        p.red(B["nrm"][0:2, tag:tag + 1], B["nrmt"][0:2, 0:S // 512],
              ALU.max)
    p.act(B["nrm_s"][0:2, 0:2], B["nrm"][0:2, 0:2], AF.Sqrt)
    p.tt(B["bneg"][0:2, 0:1], B["nrm_s"][0:2, 0:1], B["nrm_s"][0:2, 1:2],
         ALU.mult)
    p.ts(B["bneg"][0:2, 0:1], B["bneg"][0:2, 0:1], -0.125)
    p.transp(psf[1][0:1, 0:2], B["bneg"][0:2, 0:1], B["ident_f"][0:2, 0:2])
    p.copy(B["bnrow"][0:1, 0:2], psf[1][0:1, 0:2])
    p.mm(psf[1][:, 0:2], B["ones1"], B["bnrow"][0:1, 0:2])
    p.copy(B["bcol"][:, 0:2], psf[1][:, 0:2])


def const_arrays():
    ones2blk = np.zeros((128, 2), dtype=BF)
    ones2blk[0:64, 0] = 1
    ones2blk[64:128, 1] = 1
    return {
        "ident_b": np.eye(128, dtype=BF),
        "ident_f": np.eye(128, dtype=np.float32),
        "ones1": np.ones((1, 128), dtype=BF),
        "ones1f": np.ones((1, 128), dtype=np.float32),
        "onescol": np.ones((128, 1), dtype=BF),
        "ones2blk": ones2blk,
    }


def declare_consts(nc):
    d = {}
    d["ident_b"] = nc.dram_tensor("ident_b", [128, 128], BF16,
                                  kind="ExternalInput").ap()
    d["ident_f"] = nc.dram_tensor("ident_f", [128, 128], F32,
                                  kind="ExternalInput").ap()
    d["ones1"] = nc.dram_tensor("ones1", [1, 128], BF16,
                                kind="ExternalInput").ap()
    d["ones1f"] = nc.dram_tensor("ones1f", [1, 128], F32,
                                 kind="ExternalInput").ap()
    d["onescol"] = nc.dram_tensor("onescol", [128, 1], BF16,
                                  kind="ExternalInput").ap()
    d["ones2blk"] = nc.dram_tensor("ones2blk", [128, 2], BF16,
                                   kind="ExternalInput").ap()
    return d


def declare_dram_a(nc, S, x_in_kind="ExternalInput",
                   x_out_kind="ExternalOutput", consts=None):
    d = dict(consts) if consts else declare_consts(nc)
    if "x_in" not in d:
        d["x_in"] = nc.dram_tensor("x_in", [D, S], F32, kind=x_in_kind).ap()
        d["x_out"] = nc.dram_tensor("x_out", [D, S], F32,
                                    kind=x_out_kind).ap()
    for nm in ("wq", "wk", "wv", "wo"):
        d[nm] = nc.dram_tensor("a_" + nm, [D, D], F32,
                               kind="ExternalInput").ap()
    for nm in ("bq_c", "bk_c", "bv_c", "bo_c", "gamma_c", "beta_c"):
        d[nm] = nc.dram_tensor("a_" + nm, [128, NCF], F32,
                               kind="ExternalInput").ap()
    d["bv_r"] = nc.dram_tensor("a_bv_r", [1, D], F32,
                               kind="ExternalInput").ap()
    d["peT"] = nc.dram_tensor("a_peT", [D, S], F32,
                              kind="ExternalInput").ap()
    d["xa_sp"] = nc.dram_tensor("a_xa_sp", [D, S], BF16,
                                kind="Internal").ap()
    d["at_sp"] = nc.dram_tensor("a_at_sp", [D, S], BF16,
                                kind="Internal").ap()
    return d


def _col_layout(b):
    return np.ascontiguousarray(b.reshape(-1, 128).T.astype(np.float32))


def prep_a_inputs(inputs, S):
    g = lambda k: np.asarray(inputs["L0_attn_" + k])
    m = dict(const_arrays())
    m["a_wq"] = np.ascontiguousarray(g("wq"), dtype=np.float32)
    m["a_wk"] = np.ascontiguousarray(g("wk"), dtype=np.float32)
    m["a_wv"] = np.ascontiguousarray(g("wv"), dtype=np.float32)
    m["a_wo"] = np.ascontiguousarray(g("wo"), dtype=np.float32)
    m["a_bq_c"] = _col_layout(g("bq"))
    m["a_bk_c"] = _col_layout(g("bk"))
    m["a_bv_c"] = _col_layout(g("bv"))
    m["a_bo_c"] = _col_layout(g("bo"))
    m["a_gamma_c"] = _col_layout(g("gamma"))
    m["a_beta_c"] = _col_layout(g("beta"))
    m["a_bv_r"] = np.ascontiguousarray(g("bv").reshape(1, D),
                                       dtype=np.float32)
    m["a_peT"] = np.ascontiguousarray(g("pe")[0, :S, :].T,
                                      dtype=np.float32)
    return m


def build_block_a(S):
    nc = bass.Bass("TRN2", target_bir_lowering=False)
    with ExitStack() as ctx:
        dram = declare_dram_a(nc, S)
        B = make_common_bufs(nc, ctx, S)
        psf, psb = make_psum(nc, ctx)
        dram["psf"], dram["psb"] = psf, psb
        p = Prog(nc, ctx)
        load_consts(p, B, dram)
        emit_block_a(p, B, psf, psb, dram, S)
        p.finalize()
    return nc


PAIRSWAP = sum([[2 * i + 1, 2 * i] for i in range(16)], [])


def ln_stats_group(p, B, psf, src, eps):
    for c in range(NCF):
        sl = slice(512 * c, 512 * (c + 1))
        p.copy(B["xcb"], B[src][:, sl])
        p.act(B["sq2"], B["xcb"], AF.Square)
        p.mm(psf[4][0:1, :], B["onescol"], B["xcb"],
             start=(c == 0), stop=(c == NCF - 1))
        p.mm(psf[5][0:1, :], B["onescol"], B["sq2"],
             start=(c == 0), stop=(c == NCF - 1))
    row_stats(p, B, psf[4][0:1, :], psf[5][0:1, :], eps=eps)


def emit_block_b(p, B, psf, psb, dram, S):
    NQ = S // 128
    NKC = S // 512
    NTG = S // 512
    ps8 = [psf[0], psf[1], psf[2], psf[3], psf[4], psf[5],
           psb[0][:].bitcast(F32)[:, 0:512], psb[1][:].bitcast(F32)[:, 0:512]]

    p.begin_params()
    for nm in ("bq_c", "bk_c", "bo_c", "ln1w_c", "ln1b_c", "ln2w_c",
               "ln2b_c"):
        p.dma("sp", B["pc_" + nm], dram[nm])
    p.dma("sp", B["bv_row"], dram["bv_r"])
    p.dma("sp", B["gbrow"], dram["gateb_r"])
    p.end_params()
    p.begin_params()
    p.dma("sp", B["triumask"], dram["triumaskf"])
    p.dma("pool", B["expb8"], dram["exp_b"])
    p.end_params()
    init_vtm_ones(p, B, S)
    p.copy(B["bvr_hi"], B["bv_row"])
    p.tt(B["bvr_lo"], B["bv_row"], B["bvr_hi"], ALU.subtract)
    p.copy(B["gbh"], B["gbrow"])
    p.tt(B["gbl"], B["gbrow"], B["gbh"], ALU.subtract)

    for g in range(NTG):
        cols = slice(512 * g, 512 * (g + 1))
        load_group(p, B, dram["x_in"], cols)
        ln_stats_group(p, B, psf, "xg", eps=1e-5)
        bcast_rows(p, B, psf[2][:, :], B["r_row_hi"][0:1, :],
                   B["r_row_lo"][0:1, :])
        bcast_rows(p, B, psf[3][:, :], B["mr_row_hi"][0:1, :],
                   B["mr_row_lo"][0:1, :])
        p.dma("sp", B["ropeA_t"], dram["ropeA"][:, cols])
        p.dma("sp", B["ropeB_t"], dram["ropeB"][:, cols])
        for c in range(NCF):
            sl = slice(512 * c, 512 * (c + 1))
            rows = slice(128 * c, 128 * (c + 1))
            p.tt(B["t1"], B["xg"][:, sl], psf[2][:, :], ALU.mult)
            p.tt(B["xab"], B["t1"], psf[3][:, :], ALU.subtract)
            p.dma("sp", dram["bin_sp"][rows, cols], B["xab"])
            p.shuffle(B["swp"], B["xab"], PAIRSWAP)
            p.tt(B["t2"], B["xab"], B["ropeA_t"], ALU.mult)
            p.tt(B["t3"], B["swp"], B["ropeB_t"], ALU.mult)
            p.tt(B["qro"], B["t2"], B["t3"], ALU.add)
            p.dma("sp", dram["qin_sp"][rows, cols], B["qro"])
        cv_pump(8)
    p.fence()

    for pr in range(NPAIR):
        pcols = slice(128 * pr, 128 * (pr + 1))
        for wi, wn in enumerate(("wqp", "wkp", "wvp")):
            load_2d(p, B[wn][:, 0:D],
                    dram["in_wT"][:, wi * D + 128 * pr:
                                  wi * D + 128 * (pr + 1)], "pool")
        for j in range(S // 512):
            jc = slice(512 * j, 512 * (j + 1))
            load_2d(p, B["xa_tA"][:, 0:NCF * 512], dram["qin_sp"][:, jc],
                    "sp")
            load_2d(p, B["xa_tB"][:, 0:NCF * 512], dram["bin_sp"][:, jc],
                    "sp")
            for c in range(NCF):
                xac = B["xa_tA"][:, 512 * c:512 * (c + 1)]
                p.mm(psf[0], B["wqp"][:, 128 * c:128 * (c + 1)], xac,
                     start=(c == 0), stop=(c == NCF - 1))
                p.mm(psf[1], B["wkp"][:, 128 * c:128 * (c + 1)], xac,
                     start=(c == 0), stop=(c == NCF - 1))
            p.ts(B["qf"][:, jc], psf[0][:, :], B["pc_bq_c"][:, pr:pr + 1],
                 op0=ALU.add)
            p.ts(B["kf"][:, jc], psf[1][:, :], B["pc_bk_c"][:, pr:pr + 1],
                 op0=ALU.add)
            for tt_ in range(4):
                kt = 4 * j + tt_
                for c in range(NCF):
                    p.mm(psf[2][:, 0:128],
                         B["xa_tB"][:, 512 * c + 128 * tt_:
                                    512 * c + 128 * (tt_ + 1)],
                         B["wvp"][:, 128 * c:128 * (c + 1)],
                         start=(c == 0), stop=False)
                p.mm(psf[2][:, 0:128], B["ones1"],
                     B["bvr_hi"][0:1, pcols], start=False, stop=False)
                p.mm(psf[2][:, 0:128], B["ones1"],
                     B["bvr_lo"][0:1, pcols], start=False, stop=True)
                p.copy(B["vtm"][:, 132 * kt:132 * kt + 64], psf[2][:, 0:64])
                p.copy(B["vtm"][:, 132 * kt + 66:132 * kt + 130],
                       psf[2][:, 64:128])
        emit_score_bound(p, B, psf, S)
        emit_attn_qtile_T(p, B, psf, psb, dram, S, pr, causal=True)
        cv_pump(8)
    p.fence()

    load_2d(p, B["wob"][:, 0:NCF * D], dram["op_wT"][:, :], "pool")
    p.begin_params()
    for c in range(NCF):
        p.dma("sp", B["gw_s"][:, 8 * c:8 * (c + 1)],
              dram["gw_T"][128 * c:128 * (c + 1), :])
    p.end_params()
    for j in range(S // 512):
        jc = slice(512 * j, 512 * (j + 1))
        atb = B["at_t"] if j % 2 == 0 else B["qf"]
        xbb = B["xa_tB"] if j % 2 == 0 else B["kf"]
        load_2d(p, atb[:, 0:NCF * 512], dram["at_sp"][:, jc], "act")
        load_2d(p, xbb[:, 0:NCF * 512], dram["bin_sp"][:, jc], "sp")
        for oc in range(NCF):
            bank = psf[oc % 4]
            for c in range(NCF):
                p.mm(bank,
                     B["wob"][:, D * c + 128 * oc:D * c + 128 * (oc + 1)],
                     atb[:, 512 * c:512 * (c + 1)],
                     start=(c == 0), stop=(c == NCF - 1))
            p.ts(B["t1"], bank[:, :], B["pc_bo_c"][:, oc:oc + 1],
                 op0=ALU.add)
            p.tt(B["hbuf"][:, 512 * oc:512 * (oc + 1)], B["t1"],
                 xbb[:, 512 * oc:512 * (oc + 1)], ALU.add)
        ln_stats_group(p, B, psf, "hbuf", eps=1e-5)
        bcast_rows(p, B, psf[4][:, :], B["r_row_hi"][0:1, :],
                   B["r_row_lo"][0:1, :])
        bcast_rows(p, B, psf[5][:, :], B["mr_row_hi"][0:1, :],
                   B["mr_row_lo"][0:1, :])
        for oc in range(NCF):
            sl = slice(512 * oc, 512 * (oc + 1))
            rows = slice(128 * oc, 128 * (oc + 1))
            p.tt(B["t1"], B["hbuf"][:, sl], psf[4][:, :], ALU.mult)
            p.tt(B["t2"], B["t1"], psf[5][:, :], ALU.subtract)
            p.ts(B["hbuf"][:, sl], B["t2"], B["pc_ln1w_c"][:, oc:oc + 1],
                 B["pc_ln1b_c"][:, oc:oc + 1], op0=ALU.mult, op1=ALU.add)
            p.copy(B["xa_tA"][:, sl], B["hbuf"][:, sl])
            p.dma("sp", dram["x1q_sp"][rows, jc], B["xa_tA"][:, sl])
        for tt_ in range(4):
            for c in range(NCF):
                p.mm(psf[0][:, 0:8],
                     B["hbuf"][:, 512 * c + 128 * tt_:
                               512 * c + 128 * (tt_ + 1)],
                     B["gw_s"][:, 8 * c:8 * (c + 1)],
                     start=(c == 0), stop=False)
            p.mm(psf[0][:, 0:8], B["ones1f"], B["gbrow"][0:1, 0:8],
                 start=False, stop=True)
            p.copy(B["g8"][:, 0:8], psf[0][:, 0:8])
            p.red(B["gng"][:, 0:1], B["g8"][:, 0:8], ALU.max, negate=True)
            p.act(B["g8e"][:, 0:8], B["g8"][:, 0:8], AF.Exp,
                  bias=B["gng"][:, 0:1], accum=B["gs1"][:, 0:1])
            p.recip(B["gr1"][:, 0:1], B["gs1"][:, 0:1])
            p.ts(B["g8"][:, 0:8], B["g8e"][:, 0:8], B["gr1"][:, 0:1],
                 op0=ALU.mult)
            p.red(B["m1c"][:, 0:1], B["g8"][:, 0:8], ALU.max)
            p.ts(B["gk"][:, 0:8], B["g8"][:, 0:8], B["m1c"][:, 0:1],
                 op0=ALU.is_ge)
            p.stt(B["g8b"][:, 0:8], B["gk"][:, 0:8], -1e9, B["g8"][:, 0:8],
                  ALU.mult, ALU.add)
            p.red(B["m2c"][:, 0:1], B["g8b"][:, 0:8], ALU.max)
            p.ts(B["gk"][:, 0:8], B["g8"][:, 0:8], B["m2c"][:, 0:1],
                 op0=ALU.is_ge)
            p.tt(B["g8"][:, 0:8], B["g8"][:, 0:8], B["gk"][:, 0:8],
                 ALU.mult)
            p.transp(psf[1][0:8, 0:128], B["g8"][:, 0:8], B["ident_f"])
            p.copy(B["cTx"][0:8, 128 * tt_:128 * (tt_ + 1)],
                   psf[1][0:8, 0:128])
        p.dma("sp", dram["c_dram"][:, jc], B["cTx"][0:8, :])
        cv_pump(8)
    p.fence()

    xs_bufs = ["P0", "P1", "PT0", "PT1", "qf", "kf", "vtm", "at_t"]
    for j in range(S // 512):
        jc = slice(512 * j, 512 * (j + 1))
        load_2d(p, B["xa_tA"][:, 0:NCF * 512], dram["x1q_sp"][:, jc],
                "act")
        p.dma("pool", B["cT8b"], dram["c_dram"][:, jc])
        for e in range(8):
            p.dma("sp", B["crow"], dram["c_dram"][e:e + 1, jc])
            p.copy(B["crh"], B["crow"])
            p.tt(B["crl"], B["crow"], B["crh"], ALU.subtract)
            cb = psf[e % 2]
            bcast_rows(p, B, cb[:, :], B["crh"][0:1, :], B["crl"][0:1, :])
            for c in range(NCF):
                sl = slice(512 * c, 512 * (c + 1))
                p.tt(B[xs_bufs[e]][:, sl], B["xa_tA"][:, sl], cb[:, :],
                     ALU.mult)
        for e in range(8):
            wbuf = B["wob"]
            if CV is not None:
                load_2d(p, wbuf[:, 0:NCF * D], dram["exp_bf"][e],
                        "sp" if e % 2 == 0 else "act")
            else:
                load_2d(p, wbuf[:, 0:NCF * D], dram["exp_wT"][e], "pool")
            for oc in range(NCF):
                bank = ps8[oc]
                for c in range(NCF):
                    p.mm(bank,
                         wbuf[:, D * c + 128 * oc:D * c + 128 * (oc + 1)],
                         B[xs_bufs[e]][:, 512 * c:512 * (c + 1)],
                         start=(e == 0 and c == 0), stop=False)
                if e == 7:
                    p.mm(bank, B["expb8"][:, 128 * oc:128 * (oc + 1)],
                         B["cT8b"][0:8, :], start=False, stop=True)
        for oc in range(NCF):
            p.tt(B["hbuf"][:, 512 * oc:512 * (oc + 1)], ps8[oc][:, :],
                 B["xa_tA"][:, 512 * oc:512 * (oc + 1)], ALU.add)
        ln_stats_group(p, B, psf, "hbuf", eps=1e-5)
        bcast_rows(p, B, psf[4][:, :], B["r_row_hi"][0:1, :],
                   B["r_row_lo"][0:1, :])
        bcast_rows(p, B, psf[5][:, :], B["mr_row_hi"][0:1, :],
                   B["mr_row_lo"][0:1, :])
        load_group(p, B, dram["x_in"], jc)
        for oc in range(NCF):
            sl = slice(512 * oc, 512 * (oc + 1))
            rows = slice(128 * oc, 128 * (oc + 1))
            p.tt(B["t1"], B["hbuf"][:, sl], psf[4][:, :], ALU.mult)
            p.tt(B["t2"], B["t1"], psf[5][:, :], ALU.subtract)
            p.ts(B["t3"], B["t2"], B["pc_ln2w_c"][:, oc:oc + 1],
                 B["pc_ln2b_c"][:, oc:oc + 1], op0=ALU.mult, op1=ALU.add)
            p.tt(B["x1o"], B["t3"], B["xg"][:, sl], ALU.add)
            p.dma("sp", dram["x_out"][rows, jc], B["x1o"])
    p.fence()


def declare_dram_b(nc, S, x_in_kind="ExternalInput",
                   x_out_kind="ExternalOutput", consts=None):
    d = dict(consts) if consts else declare_consts(nc)
    if "x_in" not in d:
        d["x_in"] = nc.dram_tensor("x_in", [D, S], F32, kind=x_in_kind).ap()
        d["x_out"] = nc.dram_tensor("x_out", [D, S], F32,
                                    kind=x_out_kind).ap()
    d["in_wT"] = nc.dram_tensor("b_in_wT", [D, 3 * D], F32,
                                kind="ExternalInput").ap()
    d["op_wT"] = nc.dram_tensor("b_op_wT", [D, D], F32,
                                kind="ExternalInput").ap()
    d["exp_wT"] = nc.dram_tensor("b_exp_wT", [8, D, D], F32,
                                 kind="ExternalInput").ap()
    d["exp_b"] = nc.dram_tensor("b_exp_b", [8, D], F32,
                                kind="ExternalInput").ap()
    d["gw_T"] = nc.dram_tensor("b_gw_T", [D, 8], F32,
                               kind="ExternalInput").ap()
    d["gateb_r"] = nc.dram_tensor("b_gateb_r", [1, 8], F32,
                                  kind="ExternalInput").ap()
    for nm in ("bq_c", "bk_c", "bo_c", "ln1w_c", "ln1b_c", "ln2w_c",
               "ln2b_c"):
        d[nm] = nc.dram_tensor("b_" + nm, [128, NCF], F32,
                               kind="ExternalInput").ap()
    d["bv_r"] = nc.dram_tensor("b_bv_r", [1, D], F32,
                               kind="ExternalInput").ap()
    d["triumaskf"] = nc.dram_tensor("b_triuf", [128, 128], F32,
                                    kind="ExternalInput").ap()
    d["ropeA"] = nc.dram_tensor("b_ropeA", [128, S], F32,
                                kind="ExternalInput").ap()
    d["ropeB"] = nc.dram_tensor("b_ropeB", [128, S], F32,
                                kind="ExternalInput").ap()
    d["exp_bf"] = nc.dram_tensor("b_exp_bf", [8, D, D], BF16,
                                 kind="Internal").ap()
    d["bin_sp"] = nc.dram_tensor("b_bin_sp", [D, S], BF16,
                                 kind="Internal").ap()
    d["qin_sp"] = nc.dram_tensor("b_qin_sp", [D, S], BF16,
                                 kind="Internal").ap()
    d["x1q_sp"] = nc.dram_tensor("b_x1q_sp", [D, S], BF16,
                                 kind="Internal").ap()
    d["at_sp"] = nc.dram_tensor("b_at_sp", [D, S], BF16,
                                kind="Internal").ap()
    d["c_dram"] = nc.dram_tensor("b_c_dram", [8, S], F32,
                                 kind="Internal").ap()
    return d


def rope_tables(S):
    inv = (1.0 / (10000.0 ** (np.arange(0, 64, 2, dtype=np.float32)
                              / np.float32(64.0)))).astype(np.float32)
    tpos = np.arange(S, dtype=np.float32)
    freqs = tpos[:, None] * inv[None, :]
    sin = np.sin(freqs).astype(np.float32)
    cos = np.cos(freqs).astype(np.float32)
    A = np.zeros((128, S), dtype=np.float32)
    Bt = np.zeros((128, S), dtype=np.float32)
    for h in range(2):
        for jj in range(64):
            i = jj // 2
            r = 64 * h + jj
            A[r] = cos[:, i]
            Bt[r] = -sin[:, i] if jj % 2 == 0 else sin[:, i]
    return A, Bt


def prep_b_inputs(inputs, S):
    g = lambda k: np.asarray(inputs["L0_ffn_" + k])
    m = {}
    ipw = g("in_proj_w")
    ipb = g("in_proj_b")
    m["b_in_wT"] = np.ascontiguousarray(ipw.T, dtype=np.float32)
    m["b_op_wT"] = np.ascontiguousarray(g("out_proj_w").T, dtype=np.float32)
    m["b_exp_wT"] = np.ascontiguousarray(
        g("expert_w").transpose(0, 2, 1), dtype=np.float32)
    m["b_exp_b"] = np.ascontiguousarray(g("expert_b"), dtype=np.float32)
    m["b_gw_T"] = np.ascontiguousarray(g("gate_w").T, dtype=np.float32)
    m["b_gateb_r"] = np.ascontiguousarray(g("gate_b").reshape(1, 8),
                                          dtype=np.float32)
    m["b_bq_c"] = _col_layout(ipb[0:D])
    m["b_bk_c"] = _col_layout(ipb[D:2 * D])
    m["b_bv_r"] = np.ascontiguousarray(ipb[2 * D:3 * D].reshape(1, D),
                                       dtype=np.float32)
    m["b_bo_c"] = _col_layout(g("out_proj_b"))
    m["b_ln1w_c"] = _col_layout(g("ln1_w"))
    m["b_ln1b_c"] = _col_layout(g("ln1_b"))
    m["b_ln2w_c"] = _col_layout(g("ln2_w"))
    m["b_ln2b_c"] = _col_layout(g("ln2_b"))
    m["b_triuf"] = np.triu(np.ones((128, 128))).astype(np.float32)
    A, Bt = rope_tables(S)
    m["b_ropeA"] = A
    m["b_ropeB"] = Bt
    return m


def build_block_b(S):
    nc = bass.Bass("TRN2", target_bir_lowering=False)
    with ExitStack() as ctx:
        dram = declare_dram_b(nc, S)
        B = make_common_bufs(nc, ctx, S)
        B.add("gw_s", [128, 8 * NCF], F32)
        for nm, sh, dt_ in (("g8e", [128, 8], F32), ("gs1", [128, 1], F32),
                            ("gr1", [128, 1], F32), ("gng", [128, 1], F32)):
            B.add(nm, sh, dt_)
        psf, psb = make_psum(nc, ctx)
        p = Prog(nc, ctx)
        load_consts(p, B, dram)
        emit_block_b(p, B, psf, psb, dram, S)
        p.finalize()
    return nc


def add_c_bufs(B):
    B.t["phqg"] = B.t["qf"]
    B.t["phkg"] = B.t["kf"]
    B.t["pvtg"] = B.t["vtm"]
    B.t["h1g"] = B.t["P0"]
    B.t["w2s"] = B.t["P1"]
    B.add("kvst", [128, 66 * NPAIR], BF16)
    B.add("kvstf", [128, 64 * NPAIR], F32)
    B.add("phkT", [128, 128], BF16)
    B.add("Aa", [128, 128], BF16)
    B.t["Ab"] = B.t["attn_o"]
    B.add("triumask", [128, 128], F32)
    B.add("dnr", [1, 128], F32)
    B.add("dnrh", [1, 128], BF16)
    B.add("dnrl", [1, 128], BF16)
    B.t["numt"] = B.t["attn_t"]
    B.add("sstf", [128, NPAIR], F32)
    for nm in ("r1_row", "mr1_row"):
        B.add(nm, [1, 512], F32)
        B.add(nm + "_hi", [1, 512], BF16)
        B.add(nm + "_lo", [1, 512], BF16)
    B.add("u_row", [1, 512], F32)
    for nm in ("ln1s_c", "ln1bc_c", "ln2s_c", "ln2bc_c", "b2_c"):
        B.add("pc_" + nm, [128, NCF], F32)
    B.add("b1_c", [128, 32], F32)


def row_stats_c(p, B, sum_ap, sumsq_ap):
    m = B["m_row"][0:1, :]
    q = B["q_row"][0:1, :]
    t = B["t_row"][0:1, :]
    r1 = B["r1_row"][0:1, :]
    mr1 = B["mr1_row"][0:1, :]
    u = B["u_row"][0:1, :]
    R = B["r_row"][0:1, :]
    mR = B["mr_row"][0:1, :]
    p.ts(m, sum_ap, 1.0 / D)
    p.ts(q, sumsq_ap, 1.0 / D)
    p.tt(t, m, m, ALU.mult)
    p.stt(q, q, 1e-5, t, ALU.add, ALU.subtract)
    p.act(t, q, AF.Sqrt)
    p.recip(r1, t)
    p.tt(mr1, m, r1, ALU.mult)
    p.ts(u, q, -1e-5, op0=ALU.add)
    p.tt(u, u, r1, ALU.mult)
    p.tt(u, u, r1, ALU.mult)
    p.ts(u, u, 1e-6, op0=ALU.add)
    p.act(t, u, AF.Sqrt)
    p.recip(u, t)
    p.tt(R, r1, u, ALU.mult)
    p.tt(mR, m, R, ALU.mult)
    for nm in ("r_row", "mr_row", "r1_row", "mr1_row"):
        s_ = B[nm][0:1, :]
        p.copy(B[nm + "_hi"][0:1, :], s_)
        p.tt(B[nm + "_lo"][0:1, :], s_, B[nm + "_hi"][0:1, :], ALU.subtract)


def emit_block_c(p, B, psf, psb, dram, S):
    NTG = S // 512
    psb0f = psb[0][:].bitcast(F32)
    psb1f = psb[1][:].bitcast(F32)
    psf2b = psf[2][:].bitcast(BF16)
    wo_b = ("PT0", "PT1")

    def w_ap(bufs, c, ocols):
        buf = B[bufs[0]] if c < 4 else B[bufs[1]]
        cc = c % 4
        return buf[:, 1024 * cc + ocols.start:1024 * cc + ocols.stop]

    p.begin_params()
    for nm in ("ln1s_c", "ln1bc_c", "ln2s_c", "ln2bc_c", "b2_c"):
        p.dma("sp", B["pc_" + nm], dram[nm])
    p.dma("sp", B["b1_c"], dram["b1_c"])
    p.dma("sp", B["triumask"], dram["triumask"])
    p.end_params()
    p.begin_params()
    for c in range(NCF):
        buf = B[wo_b[0]] if c < 4 else B[wo_b[1]]
        p.dma("pool", buf[:, 1024 * (c % 4):1024 * (c % 4 + 1)],
              dram["wo"][128 * c:128 * (c + 1), :])
    p.end_params()
    p.memset(B["kvst"][:], 0.0, eng="dve")
    p.memset(B["kvstf"][:], 0.0, eng="dve")
    p.memset(B["sstf"][:], 0.0, eng="dve")

    for g in range(NTG):
        cols = slice(512 * g, 512 * (g + 1))
        load_group(p, B, dram["x_in"], cols)
        for c in range(NCF):
            sl = slice(512 * c, 512 * (c + 1))
            p.copy(B["xcb"], B["xg"][:, sl])
            p.act(B["sq2"], B["xcb"], AF.Square)
            p.mm(psf[4][0:1, :], B["onescol"], B["xcb"],
                 start=(c == 0), stop=(c == NCF - 1))
            p.mm(psf[5][0:1, :], B["onescol"], B["sq2"],
                 start=(c == 0), stop=(c == NCF - 1))
        row_stats_c(p, B, psf[4][0:1, :], psf[5][0:1, :])
        bcast_rows(p, B, psf[2][:, :], B["r_row_hi"][0:1, :],
                   B["r_row_lo"][0:1, :])
        bcast_rows(p, B, psf[3][:, :], B["mr_row_hi"][0:1, :],
                   B["mr_row_lo"][0:1, :])
        for c in range(NCF):
            sl = slice(512 * c, 512 * (c + 1))
            p.tt(B["t1"], B["xg"][:, sl], psf[2][:, :], ALU.mult)
            p.tt(B["t2"], B["t1"], psf[3][:, :], ALU.subtract)
            p.ts(B["xa_tA"][:, sl], B["t2"], B["pc_ln1s_c"][:, c:c + 1],
                 B["pc_ln1bc_c"][:, c:c + 1], op0=ALU.mult, op1=ALU.add)
        for pr in range(NPAIR):
            pcols = slice(128 * pr, 128 * (pr + 1))
            if CV is not None and pr % 2 == 1:
                wq_b = B["ropeA_t"][:].bitcast(BF16)
                wk_b = B["ropeB_t"][:].bitcast(BF16)
                wv_b = B["t2"][:].bitcast(BF16)
            else:
                wq_b = B["wqp"][:]
                wk_b = B["wkp"][:]
                wv_b = B["wvp"][:]
            if CV is not None:
                for wn, buf, q in (("wq", wq_b, "sp"), ("wk", wk_b, "act"),
                                   ("wv", wv_b, "sp")):
                    p.dma(q, buf[:, 0:D],
                          dram[wn + "_bf"][:, pcols].rearrange(
                              "(t p) n -> p t n", p=128))
            else:
                for c in range(NCF):
                    rows = slice(128 * c, 128 * (c + 1))
                    for wn, buf in (("wq", wq_b), ("wk", wk_b),
                                    ("wv", wv_b)):
                        p.dma("pool", buf[:, 128 * c:128 * (c + 1)],
                              dram[wn][rows, pcols])
            for c in range(NCF):
                p.mm(psf[0], wq_b[:, 128 * c:128 * (c + 1)],
                     B["xa_tA"][:, 512 * c:512 * (c + 1)],
                     start=(c == 0), stop=(c == NCF - 1))
                p.mm(psf[1], wk_b[:, 128 * c:128 * (c + 1)],
                     B["xa_tA"][:, 512 * c:512 * (c + 1)],
                     start=(c == 0), stop=(c == NCF - 1))
            p.ts(B["phqg"][:, 512 * pr:512 * (pr + 1)], psf[0][:, :],
                 0.0, 0.001, op0=ALU.max, op1=ALU.add)
            p.ts(B["phkg"][:, 512 * pr:512 * (pr + 1)], psf[1][:, :],
                 0.0, 0.001, op0=ALU.max, op1=ALU.add)
            for tt_ in range(4):
                for c in range(NCF):
                    p.mm(psf[3][:, 0:128],
                         B["xa_tA"][:, 512 * c + 128 * tt_:
                                    512 * c + 128 * (tt_ + 1)],
                         wv_b[:, 128 * c:128 * (c + 1)],
                         start=(c == 0), stop=(c == NCF - 1))
                p.copy(B["pvtg"][:, 512 * pr + 128 * tt_:
                                 512 * pr + 128 * (tt_ + 1)],
                       psf[3][:, 0:128])
        for tt_ in range(4):
            for pr in range(NPAIR):
                qsl = slice(512 * pr + 128 * tt_,
                            512 * pr + 128 * (tt_ + 1))
                kvo = 66 * pr
                p.mm(psf[0][:, 0:128], B["phkg"][0:64, qsl],
                     B["phqg"][0:64, qsl])
                p.mm(psf[1][:, 0:128], B["phkg"][64:128, qsl],
                     B["phqg"][64:128, qsl])
                p.tt(B["Aa"], psf[0][:, 0:128], B["triumask"], ALU.mult)
                p.tt(B["Ab"], psf[1][:, 0:128], B["triumask"], ALU.mult)
                p.mm(psf[4][0:64, 0:128],
                     B["pvtg"][:, qsl.start:qsl.start + 64],
                     B["Aa"], start=True, stop=False, tile_position=(0, 0))
                p.mm(psf[4][0:64, 0:128], B["kvst"][0:64, kvo:kvo + 64],
                     B["phqg"][0:64, qsl], start=False, stop=True)
                p.mm(psf[5][64:128, 0:128],
                     B["pvtg"][:, qsl.start + 64:qsl.start + 128],
                     B["Ab"], start=True, stop=False, tile_position=(0, 64))
                p.mm(psf[5][64:128, 0:128], B["kvst"][64:128, kvo:kvo + 64],
                     B["phqg"][64:128, qsl], start=False, stop=True,
                     tile_position=(64, 64))
                p.mm(psf[3][0:1, 128:256], B["onescol"], B["Aa"],
                     start=True, stop=False)
                p.mm(psf[3][0:1, 128:256], B["kvst"][0:64, kvo + 64:kvo + 65],
                     B["phqg"][0:64, qsl], start=False, stop=True)
                p.mm(psf[3][0:1, 256:384], B["onescol"], B["Ab"],
                     start=True, stop=False)
                p.mm(psf[3][0:1, 256:384],
                     B["kvst"][64:128, kvo + 64:kvo + 65],
                     B["phqg"][64:128, qsl], start=False, stop=True)
                p.transp(psf2b[:, 0:128], B["phkg"][:, qsl], B["ident_b"])
                p.copy(B["phkT"], psf2b[:, 0:128])
                kvc = slice(64 * pr, 64 * (pr + 1))
                p.mm(psb0f[0:64, kvc], B["ident_f"][0:64, 0:64],
                     B["kvstf"][0:64, kvc], start=True, stop=False,
                     tile_position=(0, 0))
                p.mm(psb0f[0:64, kvc], B["phkT"][:, 0:64],
                     B["pvtg"][:, qsl.start:qsl.start + 64],
                     start=False, stop=True, tile_position=(0, 0))
                p.mm(psb1f[64:128, kvc], B["ident_f"][64:128, 64:128],
                     B["kvstf"][64:128, kvc], start=True, stop=False,
                     tile_position=(64, 64))
                p.mm(psb1f[64:128, kvc], B["phkT"][:, 64:128],
                     B["pvtg"][:, qsl.start + 64:qsl.start + 128],
                     start=False, stop=True, tile_position=(0, 64))
                p.mm(psf[3][:, 1:2], B["phkT"], B["onescol"].bitcast(BF16)
                     if False else B["onescol"], start=True, stop=True)
                p.tt(B["sstf"][:, pr:pr + 1], B["sstf"][:, pr:pr + 1],
                     psf[3][:, 1:2], ALU.add)
                p.copy(B["kvstf"][0:64, kvc], psb0f[0:64, kvc])
                p.copy(B["kvstf"][64:128, kvc], psb1f[64:128, kvc])
                p.copy(B["kvst"][0:64, kvo:kvo + 64], psb0f[0:64, kvc])
                p.copy(B["kvst"][64:128, kvo:kvo + 64],
                       psb1f[64:128, kvc])
                p.copy(B["kvst"][0:64, kvo + 64:kvo + 65],
                       B["sstf"][0:64, pr:pr + 1])
                p.copy(B["kvst"][64:128, kvo + 64:kvo + 65],
                       B["sstf"][64:128, pr:pr + 1])
                p.recip(B["dnr"], psf[3][0:1, 128:256])
                p.copy(B["dnrh"], B["dnr"])
                p.tt(B["dnrl"], B["dnr"], B["dnrh"], ALU.subtract)
                p.mm(psf[0][0:64, 0:128], B["ones1"][0:1, 0:64],
                     B["dnrh"], start=True, stop=False, tile_position=(0, 0))
                p.mm(psf[0][0:64, 0:128], B["ones1"][0:1, 0:64],
                     B["dnrl"], start=False, stop=True, tile_position=(0, 0))
                p.recip(B["dnr"], psf[3][0:1, 256:384])
                p.copy(B["dnrh"], B["dnr"])
                p.tt(B["dnrl"], B["dnr"], B["dnrh"], ALU.subtract)
                p.mm(psf[1][64:128, 0:128], B["ones1"][0:1, 0:64],
                     B["dnrh"], start=True, stop=False,
                     tile_position=(0, 64))
                p.mm(psf[1][64:128, 0:128], B["ones1"][0:1, 0:64],
                     B["dnrl"], start=False, stop=True,
                     tile_position=(0, 64))
                p.copy(B["numt"][0:64, :], psf[4][0:64, 0:128])
                p.copy(B["numt"][64:128, :], psf[5][64:128, 0:128])
                p.tt(B["phqg"][0:64, qsl], B["numt"][0:64, :],
                     psf[0][0:64, 0:128], ALU.mult)
                p.tt(B["phqg"][64:128, qsl], B["numt"][64:128, :],
                     psf[1][64:128, 0:128], ALU.mult)
        bcast_rows(p, B, psf[2][:, :], B["r1_row_hi"][0:1, :],
                   B["r1_row_lo"][0:1, :])
        bcast_rows(p, B, psf[3][:, :], B["mr1_row_hi"][0:1, :],
                   B["mr1_row_lo"][0:1, :])
        for oc in range(NCF):
            ocols = slice(128 * oc, 128 * (oc + 1))
            bank = psf[oc % 2]
            for c in range(NCF):
                p.mm(bank, w_ap(wo_b, c, ocols),
                     B["phqg"][:, 512 * c:512 * (c + 1)],
                     start=(c == 0), stop=(c == NCF - 1))
            sl = slice(512 * oc, 512 * (oc + 1))
            p.tt(B["t1"], B["xg"][:, sl], psf[2][:, :], ALU.mult)
            p.tt(B["t2"], B["t1"], psf[3][:, :], ALU.subtract)
            p.tt(B["hbuf"][:, sl], B["t2"], bank[:, :], ALU.add)
            p.copy(B["xa_tB"][:, sl], B["hbuf"][:, sl])
        ln_stats_group(p, B, psf, "hbuf", eps=1e-6)
        bcast_rows(p, B, psf[4][:, :], B["r_row_hi"][0:1, :],
                   B["r_row_lo"][0:1, :])
        bcast_rows(p, B, psf[5][:, :], B["mr_row_hi"][0:1, :],
                   B["mr_row_lo"][0:1, :])
        for oc in range(NCF):
            sl = slice(512 * oc, 512 * (oc + 1))
            p.tt(B["t1"], B["hbuf"][:, sl], psf[4][:, :], ALU.mult)
            p.tt(B["t2"], B["t1"], psf[5][:, :], ALU.subtract)
            p.ts(B["phkg"][:, sl], B["t2"], B["pc_ln2s_c"][:, oc:oc + 1],
                 B["pc_ln2bc_c"][:, oc:oc + 1], op0=ALU.mult, op1=ALU.add)
        for b in range(4):
            if CV is not None:
                load_2d(p, B["wob"][:, 0:NCF * D],
                        dram["w1_bf"][:, 1024 * b:1024 * (b + 1)],
                        "sp" if b % 2 == 0 else "act")
            else:
                load_2d(p, B["wob"][:, 0:NCF * D],
                        dram["w1"][:, 1024 * b:1024 * (b + 1)], "pool")
            for hc in range(8):
                bank = psf[hc % 2]
                for c in range(NCF):
                    p.mm(bank,
                         B["wob"][:, 1024 * c + 128 * hc:
                                  1024 * c + 128 * (hc + 1)],
                         B["phkg"][:, 512 * c:512 * (c + 1)],
                         start=(c == 0), stop=(c == NCF - 1))
                p.act(B["h1g"][:, 512 * hc:512 * (hc + 1)], bank[:, :],
                      AF.Gelu_apprx_tanh,
                      bias=B["b1_c"][:, 8 * b + hc:8 * b + hc + 1])
            for oh in range(2):
                if CV is not None:
                    load_2d(p, B["w2s"][:, 0:8 * 512],
                            dram["w2_bf"][1024 * b:1024 * (b + 1),
                                          512 * oh:512 * (oh + 1)],
                            "act" if b % 2 == 0 else "sp")
                else:
                    load_2d(p, B["w2s"][:, 0:8 * 512],
                            dram["w2"][1024 * b:1024 * (b + 1),
                                       512 * oh:512 * (oh + 1)], "pool")
                for oc4 in range(4):
                    oc = 4 * oh + oc4
                    bank = psf[2 + oc4 % 2]
                    for hc in range(8):
                        p.mm(bank,
                             B["w2s"][:, 512 * hc + 128 * oc4:
                                      512 * hc + 128 * (oc4 + 1)],
                             B["h1g"][:, 512 * hc:512 * (hc + 1)],
                             start=(hc == 0), stop=(hc == 7))
                    sl = slice(512 * oc, 512 * (oc + 1))
                    if b == 0:
                        p.tt(B["hbuf"][:, sl], bank[:, :], B["xa_tB"][:, sl],
                             ALU.mult) if False else None
                        p.copy(B["hbuf"][:, sl], bank[:, :])
                    else:
                        p.tt(B["hbuf"][:, sl], B["hbuf"][:, sl], bank[:, :],
                             ALU.add)
        for oc in range(NCF):
            sl = slice(512 * oc, 512 * (oc + 1))
            rows = slice(128 * oc, 128 * (oc + 1))
            p.ts(B["t1"], B["hbuf"][:, sl], B["pc_b2_c"][:, oc:oc + 1],
                 op0=ALU.add)
            p.tt(B["t2"], B["t1"], B["xa_tB"][:, sl], ALU.add)
            p.tt(B["x1o"], B["t2"], B["xg"][:, sl], ALU.add)
            p.dma("sp", dram["x_out"][rows, cols], B["x1o"])
    p.fence()


def declare_dram_c(nc, S, x_in_kind="ExternalInput",
                   x_out_kind="ExternalOutput", consts=None):
    d = dict(consts) if consts else declare_consts(nc)
    if "x_in" not in d:
        d["x_in"] = nc.dram_tensor("x_in", [D, S], F32, kind=x_in_kind).ap()
        d["x_out"] = nc.dram_tensor("x_out", [D, S], F32,
                                    kind=x_out_kind).ap()
    for nm in ("wq", "wk", "wv", "wo"):
        d[nm] = nc.dram_tensor("c_" + nm, [D, D], F32,
                               kind="ExternalInput").ap()
    d["w1"] = nc.dram_tensor("c_w1", [D, 4 * D], F32,
                             kind="ExternalInput").ap()
    d["w2"] = nc.dram_tensor("c_w2", [4 * D, D], F32,
                             kind="ExternalInput").ap()
    for nm in ("ln1s_c", "ln1bc_c", "ln2s_c", "ln2bc_c", "b2_c"):
        d[nm] = nc.dram_tensor("c_" + nm, [128, NCF], F32,
                               kind="ExternalInput").ap()
    d["b1_c"] = nc.dram_tensor("c_b1_c", [128, 32], F32,
                               kind="ExternalInput").ap()
    d["triumask"] = nc.dram_tensor("c_triu", [128, 128], F32,
                                   kind="ExternalInput").ap()
    d["w1_bf"] = nc.dram_tensor("c_w1_bf", [D, 4 * D], BF16,
                                kind="Internal").ap()
    d["w2_bf"] = nc.dram_tensor("c_w2_bf", [4 * D, D], BF16,
                                kind="Internal").ap()
    for nm in ("wq", "wk", "wv"):
        d[nm + "_bf"] = nc.dram_tensor("c_" + nm + "_bf", [D, D], BF16,
                                       kind="Internal").ap()
    return d


def prep_c_inputs(inputs, S):
    g = lambda k: np.asarray(inputs["L1_attn_" + k])
    m = {}
    m["c_wq"] = np.ascontiguousarray(g("wq").reshape(D, D), dtype=np.float32)
    m["c_wk"] = np.ascontiguousarray(g("wk").reshape(D, D), dtype=np.float32)
    m["c_wv"] = np.ascontiguousarray(g("wv").reshape(D, D), dtype=np.float32)
    m["c_wo"] = np.ascontiguousarray(g("wo").reshape(D, D), dtype=np.float32)
    m["c_w1"] = np.ascontiguousarray(g("w1"), dtype=np.float32)
    m["c_w2"] = np.ascontiguousarray(g("w2"), dtype=np.float32)
    m["c_ln1s_c"] = _col_layout(g("ln1_scale"))
    m["c_ln1bc_c"] = _col_layout(g("ln1_bias"))
    m["c_ln2s_c"] = _col_layout(g("ln2_scale"))
    m["c_ln2bc_c"] = _col_layout(g("ln2_bias"))
    m["c_b2_c"] = _col_layout(g("b2"))
    m["c_b1_c"] = np.ascontiguousarray(
        g("b1").reshape(32, 128).T, dtype=np.float32)
    m["c_triu"] = np.triu(np.ones((128, 128))).astype(np.float32)
    return m


def build_block_c(S):
    nc = bass.Bass("TRN2", target_bir_lowering=False)
    with ExitStack() as ctx:
        dram = declare_dram_c(nc, S)
        B = make_common_bufs(nc, ctx, S)
        add_c_bufs(B)
        psf, psb = make_psum(nc, ctx)
        p = Prog(nc, ctx)
        load_consts(p, B, dram)
        emit_block_c(p, B, psf, psb, dram, S)
        p.finalize()
    return nc


def add_d_bufs(B):
    B.add("g32", [128, 32], F32)
    B.add("sfc", [128, 32], F32)
    B.add("sm32", [128, 32], F32)
    B.add("cur32", [128, 32], F32)
    B.add("k32", [128, 32], F32)
    B.add("cbc", [128, 512], BF16)
    B.add("gw_s32", [128, 32 * NCF], F32)
    B.add("ebc", [128, 32], F32)
    B.t["dnw"] = B.t["at_t"]
    B.t["din_d"] = B.t["xa_tA"]
    B.t["hh_d"] = B.t["P0"]
    B.t["sg_d"] = B.t["P1"]
    B.t["uu_d"] = B.t["PT0"]


def emit_block_d(p, B, psf, psb, dram, S):
    NTG = S // 512
    psb0f = psb[0][:].bitcast(F32)
    psb1f = psb[1][:].bitcast(F32)
    p.begin_params()
    p.dma("sp", B["gw_s32"], dram["gw_T32_pc"])
    p.dma("sp", B["ebc"], dram["ebc"])
    p.end_params()

    for g in range(NTG):
        cols = slice(512 * g, 512 * (g + 1))
        load_group(p, B, dram["x_in"], cols)
        ln_stats_group(p, B, psf, "xg", eps=1e-5)
        bcast_rows(p, B, psf[2][:, :], B["r_row_hi"][0:1, :],
                   B["r_row_lo"][0:1, :])
        bcast_rows(p, B, psf[3][:, :], B["mr_row_hi"][0:1, :],
                   B["mr_row_lo"][0:1, :])
        for c in range(NCF):
            sl = slice(512 * c, 512 * (c + 1))
            rows = slice(128 * c, 128 * (c + 1))
            p.tt(B["t1"], B["xg"][:, sl], psf[2][:, :], ALU.mult)
            p.tt(B["hbuf"][:, sl], B["t1"], psf[3][:, :], ALU.subtract)
            p.copy(B["xab"], B["hbuf"][:, sl])
            p.dma("sp", dram["din_sp"][rows, cols], B["xab"])
        for tt_ in range(4):
            tcols = slice(128 * tt_, 128 * (tt_ + 1))
            for c in range(NCF):
                p.mm(psf[0][:, 0:32],
                     B["hbuf"][:, 512 * c + 128 * tt_:
                               512 * c + 128 * (tt_ + 1)],
                     B["gw_s32"][:, 32 * c:32 * (c + 1)],
                     start=(c == 0), stop=(c == NCF - 1))
            p.act(B["g32"], psf[0][:, 0:32], AF.Sigmoid)
            p.tt(B["sfc"], B["g32"], B["ebc"], ALU.add)
            p.red(B["gs8"], B["sfc"][:].rearrange("p (g f) -> p g f", f=4),
                  ALU.max)
            p.copy(B["cur8"], B["gs8"])
            for i in range(4):
                p.red(B["m1c"], B["cur8"], ALU.max)
                if i < 3:
                    p.ts(B["gk"][:, 0:8], B["cur8"], B["m1c"][:, 0:1],
                         op0=ALU.is_ge)
                    p.stt(B["cur8"], B["gk"][:, 0:8], -1e9, B["cur8"],
                          ALU.mult, ALU.add)
            p.ts(B["gk"][:, 0:8], B["gs8"], B["m1c"][:, 0:1], op0=ALU.is_ge)
            for i in range(4):
                p.copy(B["sm32"][:, i:32:4], B["gk"][:, 0:8])
            p.tt(B["sm32"], B["sfc"], B["sm32"], ALU.mult)
            p.copy(B["cur32"], B["sm32"])
            for i in range(6):
                p.red(B["m2c"], B["cur32"], ALU.max)
                if i < 5:
                    p.ts(B["k32"], B["cur32"], B["m2c"][:, 0:1],
                         op0=ALU.is_ge)
                    p.stt(B["cur32"], B["k32"], -1e9, B["cur32"],
                          ALU.mult, ALU.add)
            p.ts(B["k32"], B["sm32"], B["m2c"][:, 0:1], op0=ALU.is_ge)
            p.tt(B["k32"], B["g32"], B["k32"], ALU.mult)
            p.red(B["gs1"], B["k32"], ALU.add)
            p.ts(B["gs1"], B["gs1"], 1e-20, op0=ALU.add)
            p.recip(B["gr1"], B["gs1"])
            p.ts(B["k32"], B["k32"], B["gr1"][:, 0:1], 2.5,
                 op0=ALU.mult, op1=ALU.mult)
            p.transp(psf[1][0:32, 0:128], B["k32"], B["ident_f"])
            p.copy(B["cTx"][0:32, tcols], psf[1][0:32, 0:128])
        p.dma("sp", dram["c_dram32"][:, cols], B["cTx"][0:32, :])
    p.fence()

    for j in range(S // 512):
        jc = slice(512 * j, 512 * (j + 1))
        load_2d(p, B["din_d"][:, 0:NCF * 512], dram["din_sp"][:, jc],
                "act")
        p.memset(B["hbuf"][:, 512 * 6:512 * 8], 0.0, eng="dve")
        for e in range(34):
            shared = e >= 32
            gu_buf = (B["wob"][:, 0:NCF * D] if e % 2 == 0 else
                      B["xg"][:].bitcast(BF16)[:, 0:NCF * D])
            dn_buf = (B["dnw"][:, 0:4 * D] if e % 2 == 0 else
                      B["hbuf"][:, 0:2048].bitcast(BF16)[:, 0:4 * D])
            if not shared:
                p.dma("sp", B["crow"], dram["c_dram32"][e:e + 1, jc])
                p.copy(B["crh"], B["crow"])
                p.tt(B["crl"], B["crow"], B["crh"], ALU.subtract)
                bcast_rows(p, B, psb0f[:, 0:512], B["crh"][0:1, :],
                           B["crl"][0:1, :])
                p.copy(B["cbc"], psb0f[:, 0:512])
            if CV is not None:
                src_gu = (dram["shgu_bf"][e - 32] if shared
                          else dram["gu_bf"][e])
                load_2d(p, gu_buf, src_gu, "sp" if e % 2 == 0 else "act")
                src_dn = (dram["shdn_bf"][e - 32] if shared
                          else dram["dn_bf"][e])
                load_2d(p, dn_buf, src_dn, "act" if e % 2 == 0 else "sp")
            else:
                src_gu = (dram["shgu_T"][e - 32] if shared
                          else dram["gu_T"][e])
                load_2d(p, gu_buf, src_gu, "pool")
                src_dn = (dram["shdn_T"][e - 32] if shared
                          else dram["dn_T"][e])
                load_2d(p, dn_buf, src_dn, "pool")
            for ic in range(4):
                gb = psb0f
                ub = psb1f
                for c in range(NCF):
                    p.mm(gb[:, 0:512],
                         gu_buf[:, 1024 * c + 128 * ic:
                                1024 * c + 128 * (ic + 1)],
                         B["din_d"][:, 512 * c:512 * (c + 1)],
                         start=(c == 0), stop=(c == NCF - 1))
                    p.mm(ub[:, 0:512],
                         gu_buf[:, 1024 * c + 512 + 128 * ic:
                                1024 * c + 512 + 128 * (ic + 1)],
                         B["din_d"][:, 512 * c:512 * (c + 1)],
                         start=(c == 0), stop=(c == NCF - 1))
                p.act(B["sg_d"][:, 512 * ic:512 * (ic + 1)], gb[:, 0:512],
                      AF.Silu)
                p.copy(B["uu_d"][:, 512 * ic:512 * (ic + 1)], ub[:, 0:512])
                p.tt(B["hh_d"][:, 512 * ic:512 * (ic + 1)],
                     B["sg_d"][:, 512 * ic:512 * (ic + 1)],
                     B["uu_d"][:, 512 * ic:512 * (ic + 1)], ALU.mult)
                if not shared:
                    p.tt(B["hh_d"][:, 512 * ic:512 * (ic + 1)],
                         B["hh_d"][:, 512 * ic:512 * (ic + 1)],
                         B["cbc"], ALU.mult)
            for oc in (6, 7, 0, 1, 2, 3, 4, 5):
                if oc < 6:
                    bank = psf[oc]
                    for ic in range(4):
                        p.mm(bank,
                             dn_buf[:, 1024 * ic + 128 * oc:
                                    1024 * ic + 128 * (oc + 1)],
                             B["hh_d"][:, 512 * ic:512 * (ic + 1)],
                             start=(e == 0 and ic == 0),
                             stop=(e == 33 and ic == 3))
                else:
                    bank = psb0f if oc == 6 else psb1f
                    for ic in range(4):
                        p.mm(bank[:, 0:512],
                             dn_buf[:, 1024 * ic + 128 * oc:
                                    1024 * ic + 128 * (oc + 1)],
                             B["hh_d"][:, 512 * ic:512 * (ic + 1)],
                             start=(ic == 0), stop=(ic == 3))
                    p.tt(B["hbuf"][:, 512 * oc:512 * (oc + 1)],
                         B["hbuf"][:, 512 * oc:512 * (oc + 1)],
                         bank[:, 0:512], ALU.add)
        load_group(p, B, dram["x_in"], jc)
        for oc in range(NCF):
            rows = slice(128 * oc, 128 * (oc + 1))
            if oc < 6:
                p.tt(B["x1o"], psf[oc][:, :], B["xg"][:, 512 * oc:
                                                      512 * (oc + 1)],
                     ALU.add)
            else:
                p.tt(B["x1o"], B["hbuf"][:, 512 * oc:512 * (oc + 1)],
                     B["xg"][:, 512 * oc:512 * (oc + 1)], ALU.add)
            p.dma("sp", dram["x_out"][rows, jc], B["x1o"])
    p.fence()


def declare_dram_d(nc, S, x_in_kind="ExternalInput",
                   x_out_kind="ExternalOutput", consts=None):
    d = dict(consts) if consts else declare_consts(nc)
    if "x_in" not in d:
        d["x_in"] = nc.dram_tensor("x_in", [D, S], F32, kind=x_in_kind).ap()
        d["x_out"] = nc.dram_tensor("x_out", [D, S], F32,
                                    kind=x_out_kind).ap()
    d["gw_T32_pc"] = nc.dram_tensor("d_gw_pc", [128, 32 * NCF], F32,
                                    kind="ExternalInput").ap()
    d["ebc"] = nc.dram_tensor("d_ebc", [128, 32], F32,
                              kind="ExternalInput").ap()
    d["gu_T"] = nc.dram_tensor("d_gu_T", [32, D, D], F32,
                               kind="ExternalInput").ap()
    d["dn_T"] = nc.dram_tensor("d_dn_T", [32, 512, D], F32,
                               kind="ExternalInput").ap()
    d["shgu_T"] = nc.dram_tensor("d_shgu_T", [2, D, D], F32,
                                 kind="ExternalInput").ap()
    d["shdn_T"] = nc.dram_tensor("d_shdn_T", [2, 512, D], F32,
                                 kind="ExternalInput").ap()
    d["gu_bf"] = nc.dram_tensor("d_gu_bf", [32, D, D], BF16,
                                kind="Internal").ap()
    d["dn_bf"] = nc.dram_tensor("d_dn_bf", [32, 512, D], BF16,
                                kind="Internal").ap()
    d["shgu_bf"] = nc.dram_tensor("d_shgu_bf", [2, D, D], BF16,
                                  kind="Internal").ap()
    d["shdn_bf"] = nc.dram_tensor("d_shdn_bf", [2, 512, D], BF16,
                                  kind="Internal").ap()
    d["din_sp"] = nc.dram_tensor("d_din_sp", [D, S], BF16,
                                 kind="Internal").ap()
    d["c_dram32"] = nc.dram_tensor("d_c32", [32, S], F32,
                                   kind="Internal").ap()
    return d


def prep_d_inputs(inputs, S):
    g = lambda k: np.asarray(inputs["L1_ffn_" + k])
    m = {}
    gw = g("gate_weight")
    gwT = gw.T
    m["d_gw_pc"] = np.ascontiguousarray(
        gwT.reshape(NCF, 128, 32).transpose(1, 0, 2).reshape(128, NCF * 32),
        dtype=np.float32)
    m["d_ebc"] = np.ascontiguousarray(
        np.broadcast_to(g("e_score_correction_bias")[None, :], (128, 32)),
        dtype=np.float32)
    m["d_gu_T"] = np.ascontiguousarray(
        g("experts_gate_up").transpose(0, 2, 1), dtype=np.float32)
    m["d_dn_T"] = np.ascontiguousarray(
        g("experts_down").transpose(0, 2, 1), dtype=np.float32)
    sg = g("shared_gate_w")
    su = g("shared_up_w")
    sd_ = g("shared_down_w")
    sgT = sg.T
    suT = su.T
    sdT = sd_.T
    shgu = np.zeros((2, D, D), dtype=np.float32)
    shdn = np.zeros((2, 512, D), dtype=np.float32)
    for s_ in range(2):
        shgu[s_, :, 0:512] = sgT[:, 512 * s_:512 * (s_ + 1)]
        shgu[s_, :, 512:1024] = suT[:, 512 * s_:512 * (s_ + 1)]
        shdn[s_] = sdT[512 * s_:512 * (s_ + 1), :]
    m["d_shgu_T"] = shgu
    m["d_shdn_T"] = shdn
    return m


def build_block_d(S):
    nc = bass.Bass("TRN2", target_bir_lowering=False)
    with ExitStack() as ctx:
        dram = declare_dram_d(nc, S)
        B = make_common_bufs(nc, ctx, S)
        B.add("gs8", [128, 8], F32)
        B.add("cur8", [128, 8], F32)
        for nm, sh in (("g8e", [128, 8]), ("gs1", [128, 1]),
                       ("gr1", [128, 1]), ("gng", [128, 1])):
            B.add(nm, sh, F32)
        add_d_bufs(B)
        psf, psb = make_psum(nc, ctx)
        p = Prog(nc, ctx)
        load_consts(p, B, dram)
        emit_block_d(p, B, psf, psb, dram, S)
        p.finalize()
    return nc


def kernel(**inputs):
    return kernel_fused(**inputs)


def add_bcd_small_bufs(B):
    B.add("gw_s", [128, 8 * NCF], F32)
    B.add("gs8", [128, 8], F32)
    B.add("cur8", [128, 8], F32)
    for nm in ("g8e", "gs1", "gr1", "gng"):
        B.add(nm, [128, 8] if nm == "g8e" else [128, 1], F32)


def build_fused(S):
    nc = bass.Bass("TRN2", target_bir_lowering=False)
    with ExitStack() as ctx:
        consts = declare_consts(nc)
        xin = nc.dram_tensor("x_in", [D, S], F32, kind="ExternalInput").ap()
        xout = nc.dram_tensor("x_out", [D, S], F32,
                              kind="ExternalOutput").ap()
        xs1 = nc.dram_tensor("xs1", [D, S], F32, kind="Internal").ap()
        xs2 = nc.dram_tensor("xs2", [D, S], F32, kind="Internal").ap()
        xs3 = nc.dram_tensor("xs3", [D, S], F32, kind="Internal").ap()
        dA = declare_dram_a(nc, S, consts=dict(consts, x_in=xin, x_out=xs1))
        dB = declare_dram_b(nc, S, consts=dict(consts, x_in=xs1, x_out=xs2))
        dC = declare_dram_c(nc, S, consts=dict(consts, x_in=xs2, x_out=xs3))
        dD = declare_dram_d(nc, S, consts=dict(consts, x_in=xs3,
                                               x_out=xout))
        B = make_common_bufs(nc, ctx, S)
        add_bcd_small_bufs(B)
        add_c_bufs(B)
        add_d_bufs(B)
        B.t["cvstg"] = B.t["P0"]
        psf, psb = make_psum(nc, ctx)
        p = Prog(nc, ctx)
        load_consts(p, B, consts)
        global CV
        cv = Converter(p, B)
        for e in range(8):
            cv.add(dB["exp_wT"][e], dB["exp_bf"][e])
        cv.add(dC["w1"], dC["w1_bf"])
        cv.add(dC["w2"], dC["w2_bf"])
        for nm in ("wq", "wk", "wv"):
            cv.add(dC[nm], dC[nm + "_bf"])
        for e in range(32):
            cv.add(dD["gu_T"][e], dD["gu_bf"][e])
            cv.add(dD["dn_T"][e], dD["dn_bf"][e])
        for e in range(2):
            cv.add(dD["shgu_T"][e], dD["shgu_bf"][e])
            cv.add(dD["shdn_T"][e], dD["shdn_bf"][e])
        CV = cv
        try:
            emit_block_a(p, B, psf, psb, dA, S)
            emit_block_b(p, B, psf, psb, dB, S)
            cv.done()
            emit_block_c(p, B, psf, psb, dC, S)
            emit_block_d(p, B, psf, psb, dD, S)
            p.finalize()
        finally:
            CV = None
    return nc


def kernel_fused(**inputs):
    S = 4096
    x = np.asarray(inputs["L0_attn_x"], dtype=np.float32)
    m = const_arrays()
    for prep in (prep_a_inputs, prep_b_inputs, prep_c_inputs,
                 prep_d_inputs):
        m.update(prep(inputs, S))
    nc = build_fused(S)
    in_maps = [dict(m, x_in=np.ascontiguousarray(x[b].T))
               for b in range(N_CORES)]
    res = run_bass_kernel_spmd(nc, in_maps, core_ids=list(range(N_CORES)))
    return np.ascontiguousarray(
        np.stack([np.asarray(r["x_out"]).T
                  for r in res.results]).astype(np.float32))
